# version 02
# baseline (speedup 1.0000x reference)
import numpy as np
from contextlib import ExitStack

import concourse.bass as bass
import concourse.mybir as mybir
from concourse.bass_utils import run_bass_kernel_spmd

F32 = mybir.dt.float32
BF16 = mybir.dt.bfloat16
AF = mybir.ActivationFunctionType
ALU = mybir.AluOpType
AX = mybir.AxisListType.X

MAGIC = 12582912.0
H = 1024
NH = 16
INNER = 4096
TOWN = 2048
NT = TOWN // 128
NTF = 32
EPS_LN1 = 1e-5
EPS_ATTN = 1e-6
EPS_LN2 = 1e-12
NGRP = 4
GT = NT // NGRP

DEBUG_TAPS = False
DEBUG_NAN = False
SBUF_BASE = 16512
SBUF_TOP = 229344
ENGINES = ("sync", "gpsimd", "scalar", "vector", "tensor")
SEMS = ("ldma", "stdma", "trdma", "gdma", "pe", "v", "a")


class Step:
    __slots__ = ("engine", "emit", "waits", "incs")

    def __init__(self, engine, emit, waits, incs):
        self.engine = engine
        self.emit = emit
        self.waits = waits
        self.incs = incs


class B:

    def __init__(self):
        self.steps = []
        self.n = {s: 0 for s in SEMS}

    def step(self, engine, emit, waits=(), inc=None, inc_amt=None):
        w = [(s, v) for (s, v) in waits if v and v > 0]
        incs = None
        after = None
        if inc is not None:
            if inc_amt is None:
                inc_amt = 16 if inc.endswith("dma") else 1
            self.n[inc] += inc_amt
            after = self.n[inc]
            incs = (inc, inc_amt)
        self.steps.append(Step(engine, emit, w, incs))
        return after

    def barrier(self):
        snap = [(s, self.n[s]) for s in SEMS if self.n[s] > 0]
        for e in ENGINES:
            self.steps.append(Step(e, None, list(snap), None))


class Alloc:

    def __init__(self, nc):
        self.nc = nc
        self.allocs = []

    def __call__(self, name, shape, dtype, phases, align=128):
        size = int(np.prod(shape[1:])) * mybir.dt.size(dtype)
        phases = set(phases)
        spans = sorted(
            (off, off + sz) for (_, off, sz, ph) in self.allocs if ph & phases
        )
        pos = SBUF_BASE
        for (s, e) in spans:
            if pos + size <= s:
                break
            pos = max(pos, (e + align - 1) // align * align)
        pos = (pos + align - 1) // align * align
        if pos + size > SBUF_TOP:
            raise RuntimeError(f"SBUF overflow: {name} needs {size}B at {pos}")
        self.allocs.append((name, pos, size, phases))
        return self.nc.alloc_sbuf_tensor_at(name, list(shape), dtype, offset=pos)


def build_core_program():
    nc = bass.Bass("TRN2", target_bir_lowering=False, debug=False)
    dr = {}
    dr["x_own"] = nc.dram_tensor("x_own", [TOWN, H], F32, kind="ExternalInput").ap()
    dr["x_oth"] = nc.dram_tensor("x_oth", [TOWN, H], F32, kind="ExternalInput").ap()
    for w in ("wq_t", "wk_t", "wv_t", "wd_t"):
        dr[w] = nc.dram_tensor(w, [H, H], F32, kind="ExternalInput").ap()
    dr["wg_t"] = nc.dram_tensor("wg_t", [H, INNER], F32, kind="ExternalInput").ap()
    dr["wu_t"] = nc.dram_tensor("wu_t", [H, INNER], F32, kind="ExternalInput").ap()
    dr["wdn_t"] = nc.dram_tensor("wdn_t", [INNER, H], F32, kind="ExternalInput").ap()
    for r in ("bq", "bk", "bv", "bd", "gamma", "beta"):
        dr[r] = nc.dram_tensor(r, [1, H], F32, kind="ExternalInput").ap()
    dr["out"] = nc.dram_tensor("out", [TOWN, H], F32, kind="ExternalOutput").ap()
    dr["tern_g"] = nc.dram_tensor("tern_g", [H, INNER], BF16, kind="Internal").ap()
    dr["tern_u"] = nc.dram_tensor("tern_u", [H, INNER], BF16, kind="Internal").ap()
    dr["tern_d"] = nc.dram_tensor("tern_d", [INNER, H], BF16, kind="Internal").ap()
    dr["s_sp"] = nc.dram_tensor("s_sp", [TOWN, H], BF16, kind="Internal").ap()
    dr["xn_sp"] = nc.dram_tensor("xn_sp", [TOWN, H], F32, kind="Internal").ap()
    if DEBUG_TAPS:
        dr["dbg"] = nc.dram_tensor("dbg", [12, 128, 1024], F32, kind="ExternalOutput").ap()
    if DEBUG_NAN:
        dr["dbg_s"] = nc.dram_tensor("dbg_s", [TOWN, H], BF16, kind="ExternalOutput").ap()
        dr["dbg_g"] = nc.dram_tensor("dbg_g", [TOWN, H], F32, kind="ExternalOutput").ap()
        dr["dbg_h"] = nc.dram_tensor("dbg_h", [TOWN, H], F32, kind="ExternalOutput").ap()
        dr["dbg_x"] = nc.dram_tensor("dbg_x", [TOWN, H], F32, kind="ExternalOutput").ap()
        dr["dbg_n"] = nc.dram_tensor("dbg_n", [128, 8 * TOWN], BF16, kind="ExternalOutput").ap()
        dr["dbg_c"] = nc.dram_tensor("dbg_c", [128, 8 * TOWN], BF16, kind="ExternalOutput").ap()
        dr["dbg_i"] = nc.dram_tensor("dbg_i", [128, 80], F32, kind="ExternalOutput").ap()
        dr["dbg_t3"] = nc.dram_tensor("dbg_t3", [128, 4 * 4096], F32, kind="ExternalOutput").ap()
        dr["dbg_cr"] = nc.dram_tensor("dbg_cr", [128, 8 * TOWN], BF16, kind="ExternalOutput").ap()
        dr["dbg_z"] = nc.dram_tensor("dbg_z", [16, 2 * TOWN], F32, kind="ExternalOutput").ap()
        dr["dbg_t"] = nc.dram_tensor("dbg_t", [NT, 128, 8], F32, kind="ExternalOutput").ap()

    b = B()
    al = Alloc(nc)

    ALL = ("W", "P1", "P3", "P2", "P4")
    wbig = al("wbig", [128, 8 * 4096], F32, ("W",))
    xn_f32 = al("xn_f32", [128, NT * H], F32, ("P1", "P3"))
    t3 = al("t3", [128, GT * 4096], F32, ("P4",))
    xnT = al("xnT", [128, 8 * TOWN], BF16, ("P1", "P3", "P2"))
    n1T = al("n1T", [128, 8 * TOWN], BF16, ("P3", "P2", "P4"))
    ctxT = al("ctxT", [128, 8 * TOWN], BF16, ("P2",))
    wgu = al("wgu", [128, 16 * 1024], BF16, ("P4",))
    wkv = al("wkv", [128, 16 * 1024], BF16, ("P1",))
    wq2 = al("wq2", [128, 8 * 1024], BF16, ("P2",))
    xoth = [al(f"xoth{i}", [128, H], F32, ("P1",)) for i in range(2)]
    xnb_scr = al("xnb_scr", [128, H], BF16, ("P1", "P3"))
    xnoT = [al(f"xnoT{i}", [128, 8 * 128], BF16, ("P1",)) for i in range(2)]
    ke_t = [al(f"ke{i}", [128, H], BF16, ("P1",)) for i in range(2)]
    ve_t = [al(f"ve{i}", [128, 65 * NH], BF16, ("P1",)) for i in range(2)]
    escr = al("escr", [128, TOWN], F32, ("P1", "P3", "P2"))
    act_junk = al("act_junk", [128, H], F32, ("P1", "P2"))
    qe_c = al("qe_c", [128, TOWN], BF16, ("P2",))
    zl_buf = al("zl_buf", [128, TOWN], F32, ("P2",))
    zpair = al("zpair", [128, TOWN], F32, ("P2",))
    kv_sb = al("kv_sb", [128, 520], F32, ("P1", "P2"))
    kv_bf = al("kv_bf", [128, 520], BF16, ("P2",))
    dstat = al("dstat", [128, 16], BF16, ("P2",))
    gb_bc = al("gb_bc", [128, 2 * H], F32, ("P2",))
    hs_scr = al("hs_scr", [128, H], F32, ("P2",))
    xnb_back = [al(f"xnbk{i}", [128, H], F32, ("P2",)) for i in range(2)]
    s_tile = al("s_tile", [128, H], BF16, ("P2",))
    n2tok = al("n2tok", [128, 4096], BF16, ("P4",))
    n2T = [al(f"n2T{i}", [128, 32 * 128], BF16, ("P4",)) for i in range(2)]
    tdc = [al(f"tdc{i}", [128, H], BF16, ("P4",)) for i in range(2)]
    y2_scr = al("y2_scr", [128, 1024], F32, ("P4",))
    sg_scr = al("sg_scr", [128, 1024], F32, ("P4",))
    g_scr = al("g_scr", [128, H], F32, ("P4",))
    x_rl = [al(f"xrl{i}", [128, H], F32, ("P4",)) for i in range(2)]
    s_rl = [al(f"srl{i}", [128, H], BF16, ("P4",)) for i in range(2)]
    out_t = [al(f"outt{i}", [128, H], F32, ("P4",)) for i in range(2)]
    tern_scr = al("tern_scr", [128, 4096], BF16, ("W",))
    g1_scr = al("g1_scr", [128, 4096], BF16, ("W",))
    g2_scr = al("g2_scr", [128, 4096], BF16, ("W",))
    ones1 = al("ones1", [1, 512], BF16, ALL)
    ones128 = al("ones128", [128, 128], BF16, ("W",))
    brows = al("brows", [1, 6 * H], BF16, ("W", "P1", "P3", "P2"))
    mcol = al("mcol", [128, 16], F32, ALL)
    amax1 = al("amax1", [128, NT], F32, ("P3", "P2", "P4"))
    inv1 = al("inv1", [128, NT], F32, ("P3", "P2", "P4"))
    s1c = al("s1c", [128, NT], F32, ("P3", "P2", "P4"))
    cg_all = al("cg_all", [128, NT], F32, ("P3", "P2", "P4"))
    cu_all = al("cu_all", [128, NT], F32, ("P3", "P2", "P4"))
    stt_sc = al("stt_sc", [128, 8], F32, ALL)
    cst = al("cst", [128, 4], F32, ALL)
    spv_t = al("spv", [128, 64], F32, ALL)
    spa_t = al("spa", [128, 64], F32, ALL)
    p4c = al("p4c", [128, 8], F32, ("P4",))

    with ExitStack() as ps_ctx:
        PE_ = ps_ctx.enter_context
        ps_k = PE_(nc.psum_tensor("ps_k", [128, 1024], F32))
        ps_v = PE_(nc.psum_tensor("ps_v", [128, 1024], F32))
        ps_c = PE_(nc.psum_tensor("ps_c", [128, 1024], F32))
        ps_d = PE_(nc.psum_tensor("ps_d", [128, 1024], F32))

        st = b.step
        x_own3 = dr["x_own"].rearrange("(t p) n -> t p n", p=128)
        x_oth3 = dr["x_oth"].rearrange("(t p) n -> t p n", p=128)
        out3 = dr["out"].rearrange("(t p) n -> t p n", p=128)
        s_sp3 = dr["s_sp"].rearrange("(t p) n -> t p n", p=128)
        xnT3 = xnT[:].rearrange("p (c l) -> p c l", c=8)
        n1T3 = n1T[:].rearrange("p (c l) -> p c l", c=8)
        ctxT3 = ctxT[:].rearrange("p (c l) -> p c l", c=8)

        dbg_scr = al("dbgscr", [128, 1024], F32, ALL) if DEBUG_TAPS else None

        def tap(i, get_ap, eng="vector"):
            if not DEBUG_TAPS:
                return
            waits = [(s, b.n[s]) for s in SEMS if b.n[s] > 0]
            st("vector", lambda v: v.tensor_copy(
                dbg_scr[0:get_ap().shape[0], 0:get_ap().shape[-1]], get_ap()),
               waits=waits, inc="v")
            st("sync", lambda s, i=i: s.dma_start(out=dr["dbg"][i], in_=dbg_scr[:]),
               waits=[("v", b.n["v"])], inc="stdma")

        def SPV():
            st("vector", lambda v: v.tensor_copy(spv_t[:], spv_t[:]), inc="v")

        def SPA():
            st("scalar", lambda s: s.copy(spa_t[:], spa_t[:]), inc="a")
        st("vector", lambda v: v.memset(spv_t[:], 0.0), inc="v")
        st("scalar", lambda s: s.memzero(spa_t[:]), inc="a")
        st("vector", lambda v: v.memset(ones1[:], 1.0), inc="v")
        st("vector", lambda v: v.memset(cst[:, 0:1], EPS_LN1), inc="v")
        st("vector", lambda v: v.memset(cst[:, 1:2], EPS_LN2), inc="v")
        st("vector", lambda v: v.memset(cst[:, 2:3], -0.5), inc="v")
        st("vector", lambda v: v.memset(cst[:, 3:4], -1.0), inc="v")
        st("vector", lambda v: v.memset(ones128[:], 1.0), inc="v")
        for i, r in enumerate(("bq", "bk", "bv", "bd", "gamma", "beta")):
            st("gpsimd", lambda g, i=i, r=r: g.dma_start(out=brows[0:1, H * i:H * (i + 1)], in_=dr[r]), inc="gdma")

        specs = [
            (dr["wg_t"], dr["tern_g"], 8, 4096, 1),
            (dr["wu_t"], dr["tern_u"], 8, 4096, 2),
            (dr["wdn_t"], dr["tern_d"], 32, 1024, 3),
        ]
        NW = float(H * INNER)
        v_prev_mat = 0
        g_spill_prev = 0
        for (wsrc, tdst, nch, ccols, slot) in specs:
            wsrc3 = wsrc.rearrange("(c p) n -> p c n", p=128)
            tdst3 = tdst.rearrange("(c p) n -> c p n", p=128)
            ld0 = st("sync", lambda s, wsrc3=wsrc3, nch=nch, ccols=ccols: s.dma_start(
                out=wbig[:, 0:nch * ccols].rearrange("p (c n) -> p c n", c=nch), in_=wsrc3),
                waits=[("v", v_prev_mat)], inc="ldma")
            pe_hist = {0: 0, 1: 0}
            for c in range(8 if ccols == 4096 else 32):
                par = c % 2
                ab = g1_scr if par == 0 else g2_scr
                vd = st("scalar", lambda s, c=c, ab=ab, ccols=ccols: s.activation(
                    ab[:, 0:ccols], wbig[:, c * ccols:(c + 1) * ccols], AF.Abs),
                    waits=[("ldma", ld0), ("pe", pe_hist[par])], inc="a")
                pp = 0
                for gcol in range(ccols // 512):
                    first = (c == 0 and gcol == 0)
                    last = (c == nch - 1 and gcol == ccols // 512 - 1)
                    pp = st("tensor", lambda t, ab=ab, gcol=gcol, first=first, last=last: t.matmul(
                        ps_d[:, 0:512], ones128[:], ab[:, 512 * gcol:512 * (gcol + 1)],
                        start=first, stop=last),
                        waits=[("a", vd)] if gcol == 0 else [], inc="pe")
                pe_hist[par] = pp
            pe_abs = b.n["pe"]
            vv = st("vector", lambda v: v.tensor_reduce(mcol[:, 0:1], ps_d[:, 0:512], AX, ALU.add),
                    waits=[("pe", pe_abs)], inc="v")
            SPV()
            vv = st("vector", lambda v, slot=slot: v.tensor_scalar(
                mcol[:, slot:slot + 1], mcol[:, 0:1], 1.0 / NW, 1e-5, ALU.mult, ALU.max), inc="v")
            SPV()
            vv = st("vector", lambda v, slot=slot: v.tensor_single_scalar(
                mcol[:, 3 + slot:4 + slot], mcol[:, slot:slot + 1], 0.5, ALU.mult), inc="v")
            SPV()
            vv = b.n["v"]
            for c in range(nch):
                v1 = st("vector", lambda v, c=c, ccols=ccols, slot=slot: v.tensor_single_scalar(
                    g1_scr[:, 0:ccols], wbig[:, c * ccols:(c + 1) * ccols],
                    mcol[:, 3 + slot:4 + slot], ALU.is_gt), waits=[("v", vv)], inc="v")
                v1 = st("vector", lambda v, c=c, ccols=ccols, slot=slot: v.tensor_scalar(
                    g2_scr[:, 0:ccols], wbig[:, c * ccols:(c + 1) * ccols],
                    -1.0, mcol[:, 3 + slot:4 + slot], ALU.mult, ALU.is_gt), inc="v")
                v1 = st("vector", lambda v, ccols=ccols: v.tensor_sub(
                    tern_scr[:, 0:ccols], g1_scr[:, 0:ccols], g2_scr[:, 0:ccols]),
                    waits=[("gdma", g_spill_prev)], inc="v")
                g_spill_prev = st("gpsimd", lambda g, c=c, tdst3=tdst3, ccols=ccols: g.dma_start(
                    out=tdst3[c], in_=tern_scr[:, 0:ccols]), waits=[("v", v1)], inc="gdma")
            v_prev_mat = b.n["v"]

        tap(2, lambda: mcol[:].rearrange("p n -> p n"))
        b.barrier()

        ldw1 = st("gpsimd", lambda g: g.dma_start(
            out=wkv[:, 0:8 * 1024].rearrange("p (c n) -> p c n", c=8),
            in_=dr["wk_t"].rearrange("(c p) n -> p c n", p=128)), inc="gdma")
        ldw1 = st("gpsimd", lambda g: g.dma_start(
            out=wkv[:, 8 * 1024:16 * 1024].rearrange("p (c n) -> p c n", c=8),
            in_=dr["wv_t"].rearrange("(c p) n -> p c n", p=128)), inc="gdma")
        for i in range(2):
            st("vector", lambda v, i=i: v.memset(
                ve_t[i][:].rearrange("p (h c) -> p h c", h=NH)[:, :, 64:65], 1.0), inc="v")

        kv_evac = {"k": 0, "v": 0}
        x_hist = {0: (0, 0), 1: (0, 0)}
        kevac_gate = {0: 0, 1: 0}
        prev_tr = 0
        prev_stats_v = 0
        pkv = 0
        for tt in range(NTF):
            own = tt < NT
            ti = tt if own else tt - NT
            par = tt % 2
            xa = xoth[par]
            src3 = x_own3 if own else x_oth3
            lda = st("sync", lambda s, ti=ti, xa=xa, src3=src3: s.dma_start(out=xa[:], in_=src3[ti]),
                     waits=[("v", x_hist[par][0]), ("a", x_hist[par][1])], inc="ldma")
            a1 = st("scalar", lambda s, xa=xa: s.activation(
                act_junk[:], xa[:], AF.Identity, accum_out=stt_sc[:, 0:1]),
                waits=[("ldma", lda), ("v", prev_stats_v)], inc="a")
            a1 = st("scalar", lambda s, xa=xa: s.activation(
                act_junk[:], xa[:], AF.Square, accum_out=stt_sc[:, 1:2]), inc="a")
            v1 = st("vector", lambda v: v.tensor_single_scalar(
                stt_sc[:, 2:3], stt_sc[:, 0:1], 1.0 / H, ALU.mult), waits=[("a", a1)], inc="v")
            v1 = st("vector", lambda v: v.tensor_single_scalar(
                stt_sc[:, 7:8], stt_sc[:, 1:2], 1.0 / H, ALU.mult), inc="v")
            v1 = st("vector", lambda v: v.tensor_mul(stt_sc[:, 4:5], stt_sc[:, 2:3], stt_sc[:, 2:3]), inc="v")
            SPV()
            v1 = st("vector", lambda v: v.tensor_sub(stt_sc[:, 3:4], stt_sc[:, 7:8], stt_sc[:, 4:5]), inc="v")
            prev_stats_v = v1
            a2 = st("scalar", lambda s: s.activation(
                stt_sc[:, 5:6], stt_sc[:, 3:4], AF.Ln, bias=cst[:, 0:1]), waits=[("v", v1)], inc="a")
            SPA()
            a2 = st("scalar", lambda s: s.activation(
                stt_sc[:, 6:7], stt_sc[:, 5:6], AF.Exp, scale=cst[:, 2:3]), inc="a")
            if own:
                v1 = st("vector", lambda v, xa=xa, ti=ti: v.tensor_scalar(
                    xn_f32[:, H * ti:H * (ti + 1)], xa[:], stt_sc[:, 2:3], stt_sc[:, 6:7],
                    ALU.subtract, ALU.mult), waits=[("a", a2)], inc="v")
                v1 = st("vector", lambda v, ti=ti: v.tensor_copy(
                    xnb_scr[:], xn_f32[:, H * ti:H * (ti + 1)]),
                    waits=[("trdma", prev_tr)], inc="v")
                tdst = xnT3[:, :, 128 * ti:128 * (ti + 1)]
            else:
                v1 = st("vector", lambda v, xa=xa: v.tensor_scalar(
                    xnb_scr[:], xa[:], stt_sc[:, 2:3], stt_sc[:, 6:7],
                    ALU.subtract, ALU.mult), waits=[("a", a2), ("trdma", prev_tr)], inc="v")
                tdst = xnoT[par][:].rearrange("p (c l) -> p c l", c=8)
            x_hist[par] = (v1, a1)
            prev_tr = st("scalar", lambda s, tdst=tdst: s.dma_start(
                out=tdst, in_=xnb_scr[:], transpose=True), waits=[("v", v1)], inc="trdma")
            xsrcT = xnT3[:, :, 128 * ti:128 * (ti + 1)] if own else xnoT[par][:].rearrange("p (c l) -> p c l", c=8)

            pe0 = 0
            for c in range(8):
                w0 = [("trdma", prev_tr), ("gdma", ldw1),
                      ("v", kv_evac["k"]), ("v", kv_evac["v"])] if c == 0 else []
                for (psd, wof, half) in ((ps_k, 0, 0), (ps_k, 0, 1), (ps_v, 8192, 0), (ps_v, 8192, 1)):
                    pe0 = st("tensor", lambda t, c=c, psd=psd, wof=wof, half=half, xsrcT=xsrcT: t.matmul(
                        psd[:, 512 * half:512 * (half + 1)], xsrcT[:, c, :],
                        wkv[:, wof + 1024 * c + 512 * half:wof + 1024 * c + 512 * (half + 1)],
                        start=(c == 0), stop=False), waits=w0, inc="pe")
                    w0 = []
            pe0 = st("tensor", lambda t: t.matmul(ps_k[:, 0:512], ones1[0:1, 0:128], brows[0:1, H:H + 512],
                                                  start=False, stop=True), inc="pe")
            pe0 = st("tensor", lambda t: t.matmul(ps_k[:, 512:1024], ones1[0:1, 0:128], brows[0:1, H + 512:2 * H],
                                                  start=False, stop=True), inc="pe")
            pe0 = st("tensor", lambda t: t.matmul(ps_v[:, 0:512], ones1[0:1, 0:128], brows[0:1, 2 * H:2 * H + 512],
                                                  start=False, stop=True), inc="pe")
            pe_kv = st("tensor", lambda t: t.matmul(ps_v[:, 512:1024], ones1[0:1, 0:128], brows[0:1, 2 * H + 512:3 * H],
                                                    start=False, stop=True), inc="pe")
            a3 = st("scalar", lambda s: s.activation(escr[:, 0:1024], ps_k[:], AF.Exp),
                    waits=[("pe", pe_kv)], inc="a")
            v2 = st("vector", lambda v: v.tensor_single_scalar(
                escr[:, 0:1024], escr[:, 0:1024], 1.0, ALU.min),
                waits=[("a", a3), ("pe", kevac_gate[par])], inc="v")
            v2 = st("vector", lambda v, par=par: v.scalar_tensor_tensor(
                ke_t[par][:], ps_k[:], 0.0, escr[:, 0:1024], ALU.max, ALU.add), inc="v")
            kv_evac["k"] = v2
            v3 = st("vector", lambda v, par=par: v.tensor_copy(
                ve_t[par][:].rearrange("p (h c) -> p h c", h=NH)[:, :, 0:64],
                ps_v[:].rearrange("p (h c) -> p h c", h=NH)), inc="v")
            kv_evac["v"] = v3
            for hh in range(NH):
                pr = hh // 2
                ro = 64 * (hh % 2)
                dst = ps_c if pr < 4 else ps_d
                co = 65 * (pr % 4)
                pkv = st("tensor", lambda t, hh=hh, ro=ro, dst=dst, co=co, par=par, tt=tt: t.matmul(
                    dst[ro:ro + 64, co:co + 65], ke_t[par][:, 64 * hh:64 * hh + 64],
                    ve_t[par][:, 65 * hh:65 * hh + 65],
                    start=(tt == 0), stop=(tt == NTF - 1)),
                    waits=[("v", v3)] if hh == 0 else [], inc="pe")
            kevac_gate[par] = pkv

        v4 = st("vector", lambda v: v.tensor_copy(kv_sb[:, 0:260], ps_c[:, 0:260]),
                waits=[("pe", pkv)], inc="v")
        v4 = st("vector", lambda v: v.tensor_copy(kv_sb[:, 260:520], ps_d[:, 0:260]), inc="v")

        tap(0, lambda: xn_f32[:, 0:1024])
        tap(1, lambda: kv_sb[:, 0:520])
        b.barrier()

        xn_sp3 = dr["xn_sp"].rearrange("(t p) n -> t p n", p=128)
        st("gpsimd", lambda g: g.dma_start(
            out=dr["xn_sp"].rearrange("(t p) n -> p t n", p=128),
            in_=xn_f32[:].rearrange("p (t n) -> p t n", t=NT)), inc="gdma")
        xn_sp_done = b.n["gdma"]
        prev_tr3 = 0
        for ti in range(NT):
            v5 = st("vector", lambda v, ti=ti: v.tensor_reduce(
                amax1[:, ti:ti + 1], xn_f32[:, H * ti:H * (ti + 1)], AX,
                ALU.max, apply_absolute_value=True), inc="v")
            SPV()
            v5 = st("vector", lambda v, ti=ti: v.tensor_single_scalar(
                inv1[:, ti:ti + 1], amax1[:, ti:ti + 1], 1e-5, ALU.max), inc="v")
            SPV()
            v5 = st("vector", lambda v, ti=ti: v.reciprocal(s1c[:, ti:ti + 1], inv1[:, ti:ti + 1]), inc="v")
            SPV()
            v5 = st("vector", lambda v, ti=ti: v.tensor_single_scalar(
                s1c[:, ti:ti + 1], s1c[:, ti:ti + 1], 127.0, ALU.mult), inc="v")
            v5 = st("vector", lambda v, ti=ti: v.tensor_single_scalar(
                inv1[:, ti:ti + 1], inv1[:, ti:ti + 1], 1.0 / 127.0, ALU.mult), inc="v")
            v5 = st("vector", lambda v, ti=ti: v.tensor_scalar(
                escr[:, 0:1024], xn_f32[:, H * ti:H * (ti + 1)], s1c[:, ti:ti + 1], MAGIC,
                ALU.mult, ALU.add), inc="v")
            v5 = st("vector", lambda v: v.tensor_single_scalar(
                xnb_scr[:], escr[:, 0:1024], MAGIC, ALU.subtract),
                waits=[("trdma", prev_tr3)], inc="v")
            prev_tr3 = st("scalar", lambda s, ti=ti: s.dma_start(
                out=n1T3[:, :, 128 * ti:128 * (ti + 1)], in_=xnb_scr[:], transpose=True),
                waits=[("v", v5), ("gdma", xn_sp_done)], inc="trdma")
        st("vector", lambda v: v.tensor_scalar(cg_all[:], inv1[:], mcol[:, 1:2], None, ALU.mult), inc="v")
        st("vector", lambda v: v.tensor_scalar(cu_all[:], inv1[:], mcol[:, 2:3], None, ALU.mult), inc="v")

        tap(3, lambda: n1T[:, 0:1024])
        tap(9, lambda: amax1[:, 0:NT])
        tap(10, lambda: s1c[:, 0:NT])
        b.barrier()

        ldq = st("gpsimd", lambda g: g.dma_start(
            out=wq2[:].rearrange("p (c n) -> p c n", c=8),
            in_=dr["wq_t"].rearrange("(c p) n -> p c n", p=128)), inc="gdma")
        v6 = st("vector", lambda v: v.tensor_copy(kv_bf[:], kv_sb[:]), inc="v")
        v6 = st("vector", lambda v: v.memset(dstat[:], 0.0), inc="v")
        for hh in range(NH):
            pr, ro = hh // 2, 64 * (hh % 2)
            cof = 260 * (pr // 4) + 65 * (pr % 4) + 64
            v6 = st("vector", lambda v, hh=hh, ro=ro, cof=cof: v.tensor_single_scalar(
                dstat[ro:ro + 64, hh:hh + 1], kv_sb[ro:ro + 64, cof:cof + 1], EPS_ATTN, ALU.add), inc="v")
        pe1 = 0
        for half in (0, 1):
            pe1 = st("tensor", lambda t, half=half: t.matmul(
                ps_k[:, 512 * half:512 * (half + 1)], ones1[0:1, 0:128],
                brows[0:1, 4 * H + 512 * half:4 * H + 512 * (half + 1)],
                start=True, stop=True), waits=[("v", v6)] if half == 0 else [], inc="pe")
        v6 = st("vector", lambda v: v.tensor_copy(gb_bc[:, 0:H], ps_k[:, 0:H]),
                waits=[("pe", pe1)], inc="v")
        for half in (0, 1):
            pe1 = st("tensor", lambda t, half=half: t.matmul(
                ps_k[:, 512 * half:512 * (half + 1)], ones1[0:1, 0:128],
                brows[0:1, 5 * H + 512 * half:5 * H + 512 * (half + 1)],
                start=True, stop=True), waits=[("v", v6)] if half == 0 else [], inc="pe")
        v6 = st("vector", lambda v: v.tensor_copy(gb_bc[:, H:2 * H], ps_k[:, 0:H]),
                waits=[("pe", pe1)], inc="v")

        prev_ctx_evac = 0
        a_ln_prev = 0
        zpair_free = 0
        denom_ps = lambda sub: (ps_k if sub < 2 else ps_v)
        for c in range(8):
            peq = 0
            w0 = [("gdma", ldq), ("v", prev_ctx_evac if c else v6)]
            for fin in range(8):
                for sub in range(4):
                    psd = ps_c if sub < 2 else ps_d
                    so = 512 * (sub % 2)
                    peq = st("tensor", lambda t, c=c, fin=fin, sub=sub, psd=psd, so=so: t.matmul(
                        psd[:, so:so + 512], wq2[:, 1024 * fin + 128 * c:1024 * fin + 128 * (c + 1)],
                        xnT3[:, fin, 512 * sub:512 * (sub + 1)],
                        start=(fin == 0), stop=False), waits=w0, inc="pe")
                    w0 = []
            for sub in range(4):
                psd = ps_c if sub < 2 else ps_d
                so = 512 * (sub % 2)
                peq = st("tensor", lambda t, c=c, sub=sub, psd=psd, so=so: t.matmul(
                    psd[:, so:so + 512], brows[0:1, 128 * c:128 * (c + 1)], ones1[0:1, 0:512],
                    start=False, stop=True), inc="pe")
            a4 = 0
            for sub in range(4):
                psd = ps_c if sub < 2 else ps_d
                so = 512 * (sub % 2)
                a4 = st("scalar", lambda s, sub=sub, psd=psd, so=so: s.activation(
                    escr[:, 512 * sub:512 * (sub + 1)], psd[:, so:so + 512], AF.Exp),
                    waits=[("pe", peq)] if sub == 0 else [], inc="a")
            v7 = st("vector", lambda v: v.tensor_single_scalar(
                escr[:, 0:2048], escr[:, 0:2048], 1.0, ALU.min), waits=[("a", a4)], inc="v")
            for sub in range(4):
                psd = ps_c if sub < 2 else ps_d
                so = 512 * (sub % 2)
                v7 = st("vector", lambda v, sub=sub, psd=psd, so=so: v.scalar_tensor_tensor(
                    qe_c[:, 512 * sub:512 * (sub + 1)], psd[:, so:so + 512], 0.0,
                    escr[:, 512 * sub:512 * (sub + 1)], ALU.max, ALU.add), inc="v")
            ped = 0
            for sub in range(4):
                dps = denom_ps(sub)
                so = 512 * (sub % 2)
                for half in (0, 1):
                    ped = st("tensor", lambda t, c=c, sub=sub, dps=dps, so=so, half=half: t.matmul(
                        dps[64 * half:64 * half + 64, so:so + 512],
                        dstat[:, 2 * c + half:2 * c + half + 1].broadcast_to([128, 64]),
                        qe_c[:, 512 * sub:512 * (sub + 1)], start=True, stop=True),
                        waits=[("v", v7), ("a", a_ln_prev)] if sub == 0 and half == 0 else [],
                        inc="pe")
            a_ln = 0
            for sub in range(4):
                dps = denom_ps(sub)
                so = 512 * (sub % 2)
                a_ln = st("scalar", lambda s, sub=sub, dps=dps, so=so: s.activation(
                    zl_buf[:, 512 * sub:512 * (sub + 1)], dps[:, so:so + 512], AF.Ln),
                    waits=[("pe", ped)] if sub == 0 else [], inc="a")
            a_ln_prev = a_ln
            a_z = st("scalar", lambda s: s.activation(
                zpair[:], zl_buf[:], AF.Exp, scale=cst[:, 3:4]),
                waits=[("v", zpair_free)], inc="a")
            pectx = 0
            cof = 260 * (c // 4) + 65 * (c % 4)
            for half in (0, 1):
                for sub in range(4):
                    psd = ps_c if sub < 2 else ps_d
                    so = 512 * (sub % 2)
                    pectx = st("tensor", lambda t, half=half, cof=cof, sub=sub, psd=psd, so=so: t.matmul(
                        psd[64 * half:64 * half + 64, so:so + 512],
                        kv_bf[64 * half:64 * half + 64, cof:cof + 64],
                        qe_c[64 * half:64 * half + 64, 512 * sub:512 * (sub + 1)],
                        start=True, stop=True), inc="pe")
            v8 = 0
            for sub in range(4):
                psd = ps_c if sub < 2 else ps_d
                so = 512 * (sub % 2)
                v8 = st("vector", lambda v, c=c, sub=sub, psd=psd, so=so: v.scalar_tensor_tensor(
                    ctxT3[:, c, 512 * sub:512 * (sub + 1)], psd[:, so:so + 512], 1.0,
                    zpair[:, 512 * sub:512 * (sub + 1)], ALU.mult, ALU.mult),
                    waits=[("pe", pectx), ("a", a_z)] if sub == 0 else [], inc="v")
            prev_ctx_evac = v8
            zpair_free = v8

        prev_zmul = prev_ctx_evac

        ldd = st("gpsimd", lambda g: g.dma_start(
            out=wq2[:].rearrange("p (c n) -> p c n", c=8),
            in_=dr["wd_t"].rearrange("(c p) n -> p c n", p=128)),
            waits=[("pe", b.n["pe"])], inc="gdma")

        prev_s_sp = 0
        prev_trb = 0
        hs_gate = 0
        trb_gate = {0: 0, 1: 0}
        prev_ln2_v = 0
        for ti in range(NT):
            par = ti % 2
            trb = st("sync", lambda s, ti=ti, par=par: s.dma_start(
                out=xnb_back[par][:], in_=xn_sp3[ti]),
                waits=[("v", trb_gate[par]), ("gdma", xn_sp_done)], inc="ldma")
            ped2 = 0
            w0 = [("gdma", ldd), ("v", prev_zmul), ("v", hs_gate)]
            for c in range(8):
                for half in (0, 1):
                    ped2 = st("tensor", lambda t, ti=ti, c=c, half=half: t.matmul(
                        ps_k[:, 512 * half:512 * (half + 1)],
                        ctxT3[:, c, 128 * ti:128 * (ti + 1)],
                        wq2[:, 1024 * c + 512 * half:1024 * c + 512 * (half + 1)],
                        start=(c == 0), stop=False), waits=w0, inc="pe")
                    w0 = []
            for half in (0, 1):
                ped2 = st("tensor", lambda t, half=half: t.matmul(
                    ps_k[:, 512 * half:512 * (half + 1)], ones1[0:1, 0:128],
                    brows[0:1, 3 * H + 512 * half:3 * H + 512 * (half + 1)],
                    start=False, stop=True), inc="pe")
            v10 = st("vector", lambda v, par=par: v.scalar_tensor_tensor(
                hs_scr[:], ps_k[:], 1.0, xnb_back[par][:], ALU.mult, ALU.add),
                waits=[("pe", ped2), ("ldma", trb)], inc="v")
            hs_gate = v10
            if DEBUG_NAN:
                st("gpsimd", lambda g, ti=ti: g.dma_start(
                    out=dr["dbg_h"].rearrange("(t p) n -> t p n", p=128)[ti], in_=hs_scr[:]),
                    waits=[("v", v10)], inc="gdma")
                st("gpsimd", lambda g, ti=ti, par=par: g.dma_start(
                    out=dr["dbg_x"].rearrange("(t p) n -> t p n", p=128)[ti], in_=xnb_back[par][:]),
                    waits=[("ldma", trb)], inc="gdma")
            trb_gate[par] = v10
            a6 = st("scalar", lambda s: s.activation(act_junk[:], hs_scr[:], AF.Identity,
                                                     accum_out=stt_sc[:, 0:1]),
                    waits=[("v", v10), ("v", prev_ln2_v)], inc="a")
            a6 = st("scalar", lambda s: s.activation(act_junk[:], hs_scr[:], AF.Square,
                                                     accum_out=stt_sc[:, 1:2]), inc="a")
            v10 = st("vector", lambda v: v.tensor_single_scalar(
                stt_sc[:, 2:3], stt_sc[:, 0:1], 1.0 / H, ALU.mult), waits=[("a", a6)], inc="v")
            v10 = st("vector", lambda v: v.tensor_single_scalar(
                stt_sc[:, 7:8], stt_sc[:, 1:2], 1.0 / H, ALU.mult), inc="v")
            v10 = st("vector", lambda v: v.tensor_mul(stt_sc[:, 4:5], stt_sc[:, 2:3], stt_sc[:, 2:3]), inc="v")
            SPV()
            v10 = st("vector", lambda v: v.tensor_sub(stt_sc[:, 3:4], stt_sc[:, 7:8], stt_sc[:, 4:5]), inc="v")
            prev_ln2_v = v10
            a6 = st("scalar", lambda s: s.activation(stt_sc[:, 5:6], stt_sc[:, 3:4], AF.Ln, bias=cst[:, 1:2]),
                    waits=[("v", v10)], inc="a")
            SPA()
            a6 = st("scalar", lambda s: s.activation(stt_sc[:, 6:7], stt_sc[:, 5:6], AF.Exp, scale=cst[:, 2:3]), inc="a")
            if DEBUG_NAN:
                st("gpsimd", lambda g, ti=ti: g.dma_start(out=dr["dbg_t"][ti], in_=stt_sc[:]),
                    waits=[("a", a6)], inc="gdma")
            v11 = st("vector", lambda v: v.tensor_scalar(
                hs_scr[:], hs_scr[:], stt_sc[:, 2:3], stt_sc[:, 6:7], ALU.subtract, ALU.mult),
                waits=[("a", a6)], inc="v")
            v11 = st("vector", lambda v: v.tensor_mul(hs_scr[:], hs_scr[:], gb_bc[:, 0:H]), inc="v")
            v11 = st("vector", lambda v: v.tensor_add(s_tile[:], hs_scr[:], gb_bc[:, H:2 * H]),
                     waits=[("gdma", prev_s_sp)], inc="v")
            prev_s_sp = st("gpsimd", lambda g, ti=ti: g.dma_start(out=s_sp3[ti], in_=s_tile[:]),
                           waits=[("v", v11)], inc="gdma")
            if DEBUG_NAN:
                prev_s_sp = st("gpsimd", lambda g, ti=ti: g.dma_start(
                    out=dr["dbg_s"].rearrange("(t p) n -> t p n", p=128)[ti], in_=s_tile[:]),
                    waits=[("v", v11)], inc="gdma")

        tap(4, lambda: qe_c[:, 0:1024])
        tap(5, lambda: zl_buf[:, 0:1024])
        tap(6, lambda: ctxT[:, 0:1024])
        tap(7, lambda: hs_scr[:, 0:1024])
        tap(11, lambda: dstat[:, 0:128])
        if DEBUG_NAN:
            st("gpsimd", lambda g: g.dma_start(out=dr["dbg_n"], in_=n1T[:]),
               waits=[("trdma", b.n["trdma"])], inc="gdma")
            st("gpsimd", lambda g: g.dma_start(out=dr["dbg_c"], in_=ctxT[:]),
               waits=[("v", b.n["v"])], inc="gdma")
            st("gpsimd", lambda g: g.dma_start(out=dr["dbg_i"][:, 0:16], in_=inv1[:]),
               waits=[("v", b.n["v"])], inc="gdma")
            st("gpsimd", lambda g: g.dma_start(out=dr["dbg_i"][:, 16:32], in_=cg_all[:]), inc="gdma")
            st("gpsimd", lambda g: g.dma_start(out=dr["dbg_i"][:, 32:48], in_=cu_all[:]), inc="gdma")
            st("gpsimd", lambda g: g.dma_start(out=dr["dbg_i"][:, 48:64], in_=mcol[:]), inc="gdma")
        b.barrier()

        tern_g3 = dr["tern_g"].rearrange("(c p) n -> p c n", p=128)
        tern_u3 = dr["tern_u"].rearrange("(c p) n -> p c n", p=128)
        tern_d3 = dr["tern_d"].rearrange("(c p) n -> c p n", p=128)
        prev_iq_consumed = 0
        down_used = {0: 0, 1: 0}
        evac_gate = {"a": 0, "v": 0}
        a8_hist = {0: 0, 1: 0}
        v14_hist = {0: 0, 1: 0}
        st_hist = {0: 0, 1: 0}
        ped3 = 0
        for g in range(NGRP):
            for iq in range(4):
                ldgu = st("sync", lambda s, iq=iq: s.dma_start(
                    out=wgu[:, 0:8 * 1024].rearrange("p (c n) -> p c n", c=8),
                    in_=tern_g3[:, :, 1024 * iq:1024 * (iq + 1)]),
                    waits=[("pe", prev_iq_consumed)], inc="ldma")
                ldgu = st("sync", lambda s, iq=iq: s.dma_start(
                    out=wgu[:, 8 * 1024:16 * 1024].rearrange("p (c n) -> p c n", c=8),
                    in_=tern_u3[:, :, 1024 * iq:1024 * (iq + 1)]), inc="ldma")
                for tl in range(GT):
                    ti = g * GT + tl
                    peg = 0
                    w0 = [("ldma", ldgu), ("a", evac_gate["a"]), ("v", evac_gate["v"])]
                    for c in range(8):
                        for (psd, wof) in ((ps_k, 0), (ps_v, 8192)):
                            for half in (0, 1):
                                peg = st("tensor", lambda t, ti=ti, c=c, psd=psd, wof=wof, half=half: t.matmul(
                                    psd[:, 512 * half:512 * (half + 1)],
                                    n1T3[:, c, 128 * ti:128 * (ti + 1)],
                                    wgu[:, wof + 1024 * c + 512 * half:wof + 1024 * c + 512 * (half + 1)],
                                    start=(c == 0), stop=(c == 7)), waits=w0, inc="pe")
                                w0 = []
                    if tl == GT - 1:
                        prev_iq_consumed = peg
                    a7 = st("scalar", lambda s, ti=ti: s.activation(
                        sg_scr[:], ps_k[:], AF.Silu, scale=cg_all[:, ti:ti + 1]),
                        waits=[("pe", peg), ("v", evac_gate["v"])], inc="a")
                    v12 = st("vector", lambda v, ti=ti, tl=tl, iq=iq: v.scalar_tensor_tensor(
                        t3[:, 4096 * tl + 1024 * iq:4096 * tl + 1024 * (iq + 1)],
                        ps_v[:], cu_all[:, ti:ti + 1], sg_scr[:], ALU.mult, ALU.mult),
                        waits=[("a", a7)], inc="v")
                    evac_gate = {"a": a7, "v": v12}
            if DEBUG_NAN and g == 0:
                st("gpsimd", lambda g2: g2.dma_start(out=dr["dbg_t3"], in_=t3[:]),
                   waits=[("v", b.n["v"])], inc="gdma")
            for hp in range(GT // 2):
                tr4s = {}
                for par in (0, 1):
                    tl = 2 * hp + par
                    ti = g * GT + tl
                    pc = 4 * par
                    v13 = st("vector", lambda v, tl=tl, pc=pc: v.tensor_reduce(
                        p4c[:, pc:pc + 1], t3[:, 4096 * tl:4096 * (tl + 1)], AX,
                        ALU.max, apply_absolute_value=True),
                        waits=[("a", a8_hist[par])], inc="v")
                    SPV()
                    v13 = st("vector", lambda v, pc=pc: v.tensor_single_scalar(
                        p4c[:, pc + 1:pc + 2], p4c[:, pc:pc + 1], 1e-5, ALU.max), inc="v")
                    SPV()
                    v13 = st("vector", lambda v, pc=pc: v.reciprocal(
                        p4c[:, pc + 2:pc + 3], p4c[:, pc + 1:pc + 2]), inc="v")
                    SPV()
                    v13 = st("vector", lambda v, pc=pc: v.tensor_single_scalar(
                        p4c[:, pc + 2:pc + 3], p4c[:, pc + 2:pc + 3], 127.0, ALU.mult), inc="v")
                    v13 = st("vector", lambda v, pc=pc: v.tensor_scalar(
                        p4c[:, pc + 3:pc + 4], p4c[:, pc + 1:pc + 2], mcol[:, 3:4], 1.0 / 127.0,
                        ALU.mult, ALU.mult), inc="v")
                    SPV()
                    trw = b.n["trdma"]
                    for qs in range(4):
                        v13 = st("vector", lambda v, tl=tl, pc=pc, qs=qs: v.tensor_scalar(
                            y2_scr[:], t3[:, 4096 * tl + 1024 * qs:4096 * tl + 1024 * (qs + 1)],
                            p4c[:, pc + 2:pc + 3], MAGIC, ALU.mult, ALU.add),
                            waits=[("trdma", trw)] if qs == 0 else [], inc="v")
                        v13 = st("vector", lambda v, qs=qs: v.tensor_single_scalar(
                            n2tok[:, 1024 * qs:1024 * (qs + 1)], y2_scr[:], MAGIC, ALU.subtract), inc="v")
                    tr4s[par] = st("scalar", lambda s, par=par: s.dma_start(
                        out=n2T[par][:].rearrange("p (c l) -> p c l", c=32),
                        in_=n2tok[:], transpose=True),
                        waits=[("v", v13), ("pe", down_used[par])], inc="trdma")
                ped_hist = [ped3, ped3] + [0] * 32
                ldd2 = 0
                for c in range(32):
                    cpar = c % 2
                    ldd2 = st("sync", lambda s, c=c, cpar=cpar: s.dma_start(
                        out=tdc[cpar][:], in_=tern_d3[c]),
                        waits=[("pe", ped_hist[c])], inc="ldma")
                    for par in (0, 1):
                        psd = ps_c if par == 0 else ps_d
                        for half in (0, 1):
                            ped3 = st("tensor", lambda t, c=c, cpar=cpar, par=par, half=half, psd=psd: t.matmul(
                                psd[:, 512 * half:512 * (half + 1)],
                                n2T[par][:, 128 * c:128 * (c + 1)],
                                tdc[cpar][:, 512 * half:512 * (half + 1)],
                                start=(c == 0), stop=(c == 31)),
                                waits=([("trdma", tr4s[par]), ("ldma", ldd2),
                                        ("a", a8_hist[0] if c == 0 else 0),
                                        ("a", a8_hist[1] if c == 0 else 0)]
                                       if half == 0 else []),
                                inc="pe")
                    ped_hist[c + 2] = ped3
                for par in (0, 1):
                    down_used[par] = ped3
                for par in (0, 1):
                    tl = 2 * hp + par
                    ti = g * GT + tl
                    pc = 4 * par
                    psd = ps_c if par == 0 else ps_d
                    ldx = st("sync", lambda s, ti=ti, par=par: s.dma_start(out=x_rl[par][:], in_=x_own3[ti]),
                             waits=[("v", v14_hist[par])], inc="ldma")
                    lds = st("gpsimd", lambda g2, ti=ti, par=par: g2.dma_start(out=s_rl[par][:], in_=s_sp3[ti]),
                             waits=[("v", v14_hist[par])], inc="gdma")
                    a8 = st("scalar", lambda s, par=par, pc=pc, psd=psd: s.activation(
                        g_scr[:], psd[:], AF.Silu, scale=p4c[:, pc + 3:pc + 4]),
                        waits=[("pe", ped3), ("v", v14_hist[0]), ("v", v14_hist[1])], inc="a")
                    a8_hist[par] = a8
                    if DEBUG_NAN:
                        st("gpsimd", lambda g2, ti=ti: g2.dma_start(
                            out=dr["dbg_g"].rearrange("(t p) n -> t p n", p=128)[ti], in_=g_scr[:]),
                            waits=[("a", a8)], inc="gdma")
                    v14 = st("vector", lambda v, par=par: v.tensor_mul(g_scr[:], g_scr[:], s_rl[par][:]),
                             waits=[("a", a8), ("gdma", lds)], inc="v")
                    v14 = st("vector", lambda v, par=par: v.tensor_add(out_t[par][:], g_scr[:], x_rl[par][:]),
                             waits=[("ldma", ldx), ("stdma", st_hist[par])], inc="v")
                    v14_hist[par] = v14
                    st_hist[par] = st("sync", lambda s, ti=ti, par=par: s.dma_start(
                        out=out3[ti], in_=out_t[par][:]), waits=[("v", v14)], inc="stdma")

        tap(8, lambda: t3[:, 0:1024])
        b.steps.append(Step("sync", None, [("stdma", b.n["stdma"])], None))

        emit(b, nc)
    return nc


def emit(b, nc):
    sem_handles = {}
    with ExitStack() as ctx:
        for s in SEMS:
            sem_handles[s] = ctx.enter_context(nc.semaphore(s))
        with nc.Block() as block:
            for name in ENGINES:
                section = getattr(block, name)

                def body(eng, name=name):
                    for stp in b.steps:
                        if stp.engine != name:
                            continue
                        for (s, val) in stp.waits:
                            eng.wait_ge(sem_handles[s], val)
                        if stp.emit is None:
                            continue
                        inst = stp.emit(eng)
                        if stp.incs is not None and inst is not None:
                            inst.then_inc(sem_handles[stp.incs[0]], stp.incs[1])

                section(body)


def kernel(**inputs):
    x = np.ascontiguousarray(np.asarray(inputs["L0_attn_input_tensor"], dtype=np.float32))
    row = lambda a: np.ascontiguousarray(np.asarray(a, np.float32)).reshape(1, -1)
    tr = lambda a: np.ascontiguousarray(np.asarray(a, np.float32).T)
    shared = {
        "wq_t": tr(inputs["L0_attn_Wq"]), "wk_t": tr(inputs["L0_attn_Wk"]),
        "wv_t": tr(inputs["L0_attn_Wv"]), "wd_t": tr(inputs["L0_attn_Wd"]),
        "wg_t": tr(inputs["L0_ffn_w_gate"]), "wu_t": tr(inputs["L0_ffn_w_up"]),
        "wdn_t": tr(inputs["L0_ffn_w_down"]),
        "bq": row(inputs["L0_attn_bq"]), "bk": row(inputs["L0_attn_bk"]),
        "bv": row(inputs["L0_attn_bv"]), "bd": row(inputs["L0_attn_bd"]),
        "gamma": row(inputs["L0_attn_gamma"]), "beta": row(inputs["L0_attn_beta"]),
    }
    in_maps = []
    for c in range(8):
        n, h = c // 2, c % 2
        m = dict(shared)
        m["x_own"] = np.ascontiguousarray(x[n, h * TOWN:(h + 1) * TOWN])
        m["x_oth"] = np.ascontiguousarray(x[n, (1 - h) * TOWN:(2 - h) * TOWN])
        in_maps.append(m)
    nc = build_core_program()
    res = run_bass_kernel_spmd(nc, in_maps, core_ids=list(range(8)))
    out = np.empty((4, 4096, 1024), np.float32)
    for c in range(8):
        n, h = c // 2, c % 2
        out[n, h * TOWN:(h + 1) * TOWN] = res.results[c]["out"]
    return out


# version 03 (final)
# speedup vs baseline: 1.0799x; 1.0799x over previous
import numpy as np
from contextlib import ExitStack

import concourse.bass as bass
import concourse.mybir as mybir
from concourse.bass_utils import run_bass_kernel_spmd

F32 = mybir.dt.float32
BF16 = mybir.dt.bfloat16
AF = mybir.ActivationFunctionType
ALU = mybir.AluOpType
AX = mybir.AxisListType.X

MAGIC = 12582912.0
H = 1024
NH = 16
INNER = 4096
TOWN = 2048
NT = TOWN // 128
NTF = 32
EPS_LN1 = 1e-5
EPS_ATTN = 1e-6
EPS_LN2 = 1e-12
NGRP = 4
GT = NT // NGRP

DEBUG_TAPS = False
DEBUG_NAN = False
SBUF_BASE = 16512
SBUF_TOP = 229344
ENGINES = ("sync", "gpsimd", "scalar", "vector", "tensor")
SEMS = ("ldma", "stdma", "trdma", "gdma", "pe", "v", "a")


class Step:
    __slots__ = ("engine", "emit", "waits", "incs")

    def __init__(self, engine, emit, waits, incs):
        self.engine = engine
        self.emit = emit
        self.waits = waits
        self.incs = incs


class B:

    def __init__(self):
        self.steps = []
        self.n = {s: 0 for s in SEMS}

    def step(self, engine, emit, waits=(), inc=None, inc_amt=None):
        w = [(s, v) for (s, v) in waits if v and v > 0]
        incs = None
        after = None
        if inc is not None:
            if inc_amt is None:
                inc_amt = 16 if inc.endswith("dma") else 1
            self.n[inc] += inc_amt
            after = self.n[inc]
            incs = (inc, inc_amt)
        self.steps.append(Step(engine, emit, w, incs))
        return after

    def barrier(self):
        snap = [(s, self.n[s]) for s in SEMS if self.n[s] > 0]
        for e in ENGINES:
            self.steps.append(Step(e, None, list(snap), None))


class Alloc:

    def __init__(self, nc):
        self.nc = nc
        self.allocs = []

    def __call__(self, name, shape, dtype, phases, align=128):
        size = int(np.prod(shape[1:])) * mybir.dt.size(dtype)
        phases = set(phases)
        spans = sorted(
            (off, off + sz) for (_, off, sz, ph) in self.allocs if ph & phases
        )
        pos = SBUF_BASE
        for (s, e) in spans:
            if pos + size <= s:
                break
            pos = max(pos, (e + align - 1) // align * align)
        pos = (pos + align - 1) // align * align
        if pos + size > SBUF_TOP:
            raise RuntimeError(f"SBUF overflow: {name} needs {size}B at {pos}")
        self.allocs.append((name, pos, size, phases))
        return self.nc.alloc_sbuf_tensor_at(name, list(shape), dtype, offset=pos)


def build_core_program():
    nc = bass.Bass("TRN2", target_bir_lowering=False, debug=False)
    dr = {}
    dr["x_own"] = nc.dram_tensor("x_own", [TOWN, H], F32, kind="ExternalInput").ap()
    dr["x_oth"] = nc.dram_tensor("x_oth", [TOWN, H], F32, kind="ExternalInput").ap()
    for w in ("wq_t", "wk_t", "wv_t", "wd_t"):
        dr[w] = nc.dram_tensor(w, [H, H], F32, kind="ExternalInput").ap()
    dr["wg_t"] = nc.dram_tensor("wg_t", [H, INNER], F32, kind="ExternalInput").ap()
    dr["wu_t"] = nc.dram_tensor("wu_t", [H, INNER], F32, kind="ExternalInput").ap()
    dr["wdn_t"] = nc.dram_tensor("wdn_t", [INNER, H], F32, kind="ExternalInput").ap()
    for r in ("bq", "bk", "bv", "bd", "gamma", "beta"):
        dr[r] = nc.dram_tensor(r, [1, H], F32, kind="ExternalInput").ap()
    dr["out"] = nc.dram_tensor("out", [TOWN, H], F32, kind="ExternalOutput").ap()
    dr["tern_g"] = nc.dram_tensor("tern_g", [H, INNER], BF16, kind="Internal").ap()
    dr["tern_u"] = nc.dram_tensor("tern_u", [H, INNER], BF16, kind="Internal").ap()
    dr["tern_d"] = nc.dram_tensor("tern_d", [INNER, H], BF16, kind="Internal").ap()
    dr["s_sp"] = nc.dram_tensor("s_sp", [TOWN, H], BF16, kind="Internal").ap()
    dr["xn_sp"] = nc.dram_tensor("xn_sp", [TOWN, H], F32, kind="Internal").ap()
    if DEBUG_TAPS:
        dr["dbg"] = nc.dram_tensor("dbg", [12, 128, 1024], F32, kind="ExternalOutput").ap()
    if DEBUG_NAN:
        dr["dbg_s"] = nc.dram_tensor("dbg_s", [TOWN, H], BF16, kind="ExternalOutput").ap()
        dr["dbg_g"] = nc.dram_tensor("dbg_g", [TOWN, H], F32, kind="ExternalOutput").ap()
        dr["dbg_h"] = nc.dram_tensor("dbg_h", [TOWN, H], F32, kind="ExternalOutput").ap()
        dr["dbg_x"] = nc.dram_tensor("dbg_x", [TOWN, H], F32, kind="ExternalOutput").ap()
        dr["dbg_n"] = nc.dram_tensor("dbg_n", [128, 8 * TOWN], BF16, kind="ExternalOutput").ap()
        dr["dbg_c"] = nc.dram_tensor("dbg_c", [128, 8 * TOWN], BF16, kind="ExternalOutput").ap()
        dr["dbg_i"] = nc.dram_tensor("dbg_i", [128, 80], F32, kind="ExternalOutput").ap()
        dr["dbg_t3"] = nc.dram_tensor("dbg_t3", [128, 4 * 4096], F32, kind="ExternalOutput").ap()
        dr["dbg_cr"] = nc.dram_tensor("dbg_cr", [128, 8 * TOWN], BF16, kind="ExternalOutput").ap()
        dr["dbg_z"] = nc.dram_tensor("dbg_z", [16, 2 * TOWN], F32, kind="ExternalOutput").ap()
        dr["dbg_t"] = nc.dram_tensor("dbg_t", [NT, 128, 8], F32, kind="ExternalOutput").ap()

    b = B()
    al = Alloc(nc)

    ALL = ("W", "P1", "P3", "P2", "P4")
    wbig = al("wbig", [128, 8 * 4096], F32, ("W",))
    xn_f32 = al("xn_f32", [128, NT * H], F32, ("P1", "P3"))
    t3 = al("t3", [128, GT * 4096], F32, ("P4",))
    xnT = al("xnT", [128, 8 * TOWN], BF16, ("P1", "P3", "P2"))
    n1T = al("n1T", [128, 8 * TOWN], BF16, ("P3", "P2", "P4"))
    ctxT = al("ctxT", [128, 8 * TOWN], BF16, ("P2",))
    wgu = al("wgu", [128, 16 * 1024], BF16, ("P4",))
    wkv = al("wkv", [128, 16 * 1024], BF16, ("P1",))
    wq2 = al("wq2", [128, 8 * 1024], BF16, ("P2",))
    xoth = [al(f"xoth{i}", [128, H], F32, ("P1",)) for i in range(2)]
    xnb_scr2 = [al(f"xnb_scr{i}", [128, H], BF16, ("P1", "P3")) for i in range(2)]
    xnoT = [al(f"xnoT{i}", [128, 8 * 128], BF16, ("P1",)) for i in range(2)]
    ke_t = [al(f"ke{i}", [128, H], BF16, ("P1",)) for i in range(2)]
    ve_t = [al(f"ve{i}", [128, 65 * NH], BF16, ("P1",)) for i in range(2)]
    escr = al("escr", [128, TOWN], F32, ("P1", "P3", "P2"))
    act_junk = al("act_junk", [128, H], F32, ("P1", "P2", "P4"))
    qe_c = al("qe_c", [128, TOWN], BF16, ("P2",))
    zl_buf = al("zl_buf", [128, TOWN], F32, ("P2",))
    zpair = al("zpair", [128, TOWN], F32, ("P2",))
    kv_sb = al("kv_sb", [128, 520], F32, ("P1", "P2"))
    kv_bf = al("kv_bf", [128, 520], BF16, ("P2",))
    dstat = al("dstat", [128, 16], BF16, ("P2",))
    gb_bc = al("gb_bc", [128, 2 * H], F32, ("P2",))
    hs_scr2 = [al(f"hs_scr{i}", [128, H], F32, ("P2", "P4")) for i in range(2)]
    xnb_back = [al(f"xnbk{i}", [128, H], F32, ("P2", "P4")) for i in range(2)]
    s_tile2 = [al(f"s_tile{i}", [128, H], BF16, ("P2", "P4")) for i in range(2)]
    n2tok = al("n2tok", [128, 4096], BF16, ("P4",))
    n2T = [al(f"n2T{i}", [128, 32 * 128], BF16, ("P4",)) for i in range(2)]
    tdc = [al(f"tdc{i}", [128, 2 * H], BF16, ("P4",)) for i in range(2)]
    y2_scr = al("y2_scr", [128, 1024], F32, ("P4",))
    sg_scr = al("sg_scr", [128, 1024], F32, ("P4",))
    tern_scr = al("tern_scr", [128, 4096], BF16, ("W",))
    g1_scr = al("g1_scr", [128, 4096], BF16, ("W",))
    g2_scr = al("g2_scr", [128, 4096], BF16, ("W",))
    ones1 = al("ones1", [1, 512], BF16, ALL)
    ones128 = al("ones128", [128, 128], BF16, ("W",))
    brows = al("brows", [1, 6 * H], BF16, ("W", "P1", "P3", "P2"))
    mcol = al("mcol", [128, 16], F32, ALL)
    amax1 = al("amax1", [128, NT], F32, ("P3", "P2", "P4"))
    inv1 = al("inv1", [128, NT], F32, ("P3", "P2", "P4"))
    s1c = al("s1c", [128, NT], F32, ("P3", "P2", "P4"))
    cg_all = al("cg_all", [128, NT], F32, ("P3", "P2", "P4"))
    cu_all = al("cu_all", [128, NT], F32, ("P3", "P2", "P4"))
    stt_sc = al("stt_sc", [128, 16], F32, ALL)
    cst = al("cst", [128, 4], F32, ALL)
    spv_t = al("spv", [128, 64], F32, ALL)
    spa_t = al("spa", [128, 64], F32, ALL)
    p4c = al("p4c", [128, 8], F32, ("P4",))

    with ExitStack() as ps_ctx:
        PE_ = ps_ctx.enter_context
        ps_k = PE_(nc.psum_tensor("ps_k", [128, 1024], F32))
        ps_v = PE_(nc.psum_tensor("ps_v", [128, 1024], F32))
        ps_c = PE_(nc.psum_tensor("ps_c", [128, 1024], F32))
        ps_d = PE_(nc.psum_tensor("ps_d", [128, 1024], F32))

        st = b.step
        x_own3 = dr["x_own"].rearrange("(t p) n -> t p n", p=128)
        x_oth3 = dr["x_oth"].rearrange("(t p) n -> t p n", p=128)
        out3 = dr["out"].rearrange("(t p) n -> t p n", p=128)
        s_sp3 = dr["s_sp"].rearrange("(t p) n -> t p n", p=128)
        xnT3 = xnT[:].rearrange("p (c l) -> p c l", c=8)
        n1T3 = n1T[:].rearrange("p (c l) -> p c l", c=8)
        ctxT3 = ctxT[:].rearrange("p (c l) -> p c l", c=8)

        dbg_scr = al("dbgscr", [128, 1024], F32, ALL) if DEBUG_TAPS else None

        def tap(i, get_ap, eng="vector"):
            if not DEBUG_TAPS:
                return
            waits = [(s, b.n[s]) for s in SEMS if b.n[s] > 0]
            st("vector", lambda v: v.tensor_copy(
                dbg_scr[0:get_ap().shape[0], 0:get_ap().shape[-1]], get_ap()),
               waits=waits, inc="v")
            st("sync", lambda s, i=i: s.dma_start(out=dr["dbg"][i], in_=dbg_scr[:]),
               waits=[("v", b.n["v"])], inc="stdma")

        def SPV():
            st("vector", lambda v: v.tensor_copy(spv_t[:], spv_t[:]), inc="v")

        def SPA():
            st("scalar", lambda s: s.copy(spa_t[:], spa_t[:]), inc="a")
        st("vector", lambda v: v.memset(spv_t[:], 0.0), inc="v")
        st("scalar", lambda s: s.memzero(spa_t[:]), inc="a")
        st("vector", lambda v: v.memset(ones1[:], 1.0), inc="v")
        st("vector", lambda v: v.memset(cst[:, 0:1], EPS_LN1), inc="v")
        st("vector", lambda v: v.memset(cst[:, 1:2], EPS_LN2), inc="v")
        st("vector", lambda v: v.memset(cst[:, 2:3], -0.5), inc="v")
        st("vector", lambda v: v.memset(cst[:, 3:4], -1.0), inc="v")
        st("vector", lambda v: v.memset(ones128[:], 1.0), inc="v")
        for i, r in enumerate(("bq", "bk", "bv", "bd", "gamma", "beta")):
            st("gpsimd", lambda g, i=i, r=r: g.dma_start(out=brows[0:1, H * i:H * (i + 1)], in_=dr[r]), inc="gdma")

        specs = [
            (dr["wg_t"], dr["tern_g"], 8, 4096, 1),
            (dr["wu_t"], dr["tern_u"], 8, 4096, 2),
            (dr["wdn_t"], dr["tern_d"], 32, 1024, 3),
        ]
        NW = float(H * INNER)
        v_prev_mat = 0
        g_spill_prev = 0
        for (wsrc, tdst, nch, ccols, slot) in specs:
            wsrc3 = wsrc.rearrange("(c p) n -> p c n", p=128)
            tdst3 = tdst.rearrange("(c p) n -> c p n", p=128)
            ld0 = st("sync", lambda s, wsrc3=wsrc3, nch=nch, ccols=ccols: s.dma_start(
                out=wbig[:, 0:nch * ccols].rearrange("p (c n) -> p c n", c=nch), in_=wsrc3),
                waits=[("v", v_prev_mat)], inc="ldma")
            pe_hist = {0: 0, 1: 0}
            for c in range(8 if ccols == 4096 else 32):
                par = c % 2
                ab = g1_scr if par == 0 else g2_scr
                vd = st("scalar", lambda s, c=c, ab=ab, ccols=ccols: s.activation(
                    ab[:, 0:ccols], wbig[:, c * ccols:(c + 1) * ccols], AF.Abs),
                    waits=[("ldma", ld0), ("pe", pe_hist[par])], inc="a")
                pp = 0
                for gcol in range(ccols // 512):
                    first = (c == 0 and gcol == 0)
                    last = (c == nch - 1 and gcol == ccols // 512 - 1)
                    pp = st("tensor", lambda t, ab=ab, gcol=gcol, first=first, last=last: t.matmul(
                        ps_d[:, 0:512], ones128[:], ab[:, 512 * gcol:512 * (gcol + 1)],
                        start=first, stop=last),
                        waits=[("a", vd)] if gcol == 0 else [], inc="pe")
                pe_hist[par] = pp
            pe_abs = b.n["pe"]
            vv = st("vector", lambda v: v.tensor_reduce(mcol[:, 0:1], ps_d[:, 0:512], AX, ALU.add),
                    waits=[("pe", pe_abs)], inc="v")
            SPV()
            vv = st("vector", lambda v, slot=slot: v.tensor_scalar(
                mcol[:, slot:slot + 1], mcol[:, 0:1], 1.0 / NW, 1e-5, ALU.mult, ALU.max), inc="v")
            SPV()
            vv = st("vector", lambda v, slot=slot: v.tensor_single_scalar(
                mcol[:, 3 + slot:4 + slot], mcol[:, slot:slot + 1], 0.5, ALU.mult), inc="v")
            SPV()
            vv = b.n["v"]
            for c in range(nch):
                v1 = st("vector", lambda v, c=c, ccols=ccols, slot=slot: v.tensor_single_scalar(
                    g1_scr[:, 0:ccols], wbig[:, c * ccols:(c + 1) * ccols],
                    mcol[:, 3 + slot:4 + slot], ALU.is_gt), waits=[("v", vv)], inc="v")
                v1 = st("vector", lambda v, c=c, ccols=ccols, slot=slot: v.tensor_scalar(
                    g2_scr[:, 0:ccols], wbig[:, c * ccols:(c + 1) * ccols],
                    -1.0, mcol[:, 3 + slot:4 + slot], ALU.mult, ALU.is_gt), inc="v")
                v1 = st("vector", lambda v, ccols=ccols: v.tensor_sub(
                    tern_scr[:, 0:ccols], g1_scr[:, 0:ccols], g2_scr[:, 0:ccols]),
                    waits=[("gdma", g_spill_prev)], inc="v")
                g_spill_prev = st("gpsimd", lambda g, c=c, tdst3=tdst3, ccols=ccols: g.dma_start(
                    out=tdst3[c], in_=tern_scr[:, 0:ccols]), waits=[("v", v1)], inc="gdma")
            v_prev_mat = b.n["v"]

        tap(2, lambda: mcol[:].rearrange("p n -> p n"))
        b.barrier()

        ldw1 = st("gpsimd", lambda g: g.dma_start(
            out=wkv[:, 0:8 * 1024].rearrange("p (c n) -> p c n", c=8),
            in_=dr["wk_t"].rearrange("(c p) n -> p c n", p=128)), inc="gdma")
        ldw1 = st("gpsimd", lambda g: g.dma_start(
            out=wkv[:, 8 * 1024:16 * 1024].rearrange("p (c n) -> p c n", c=8),
            in_=dr["wv_t"].rearrange("(c p) n -> p c n", p=128)), inc="gdma")
        for i in range(2):
            st("vector", lambda v, i=i: v.memset(
                ve_t[i][:].rearrange("p (h c) -> p h c", h=NH)[:, :, 64:65], 1.0), inc="v")

        kv_evac = {"k": 0, "v": 0}
        x_hist = {0: (0, 0), 1: (0, 0)}
        kevac_gate = {0: 0, 1: 0}
        tr_hist = {0: 0, 1: 0}
        xnoT_pe = {0: 0, 1: 0}
        stats_hist = {0: 0, 1: 0}
        pkv = 0
        for tt in range(NTF):
            own = tt < NT
            ti = tt if own else tt - NT
            par = tt % 2
            sb = 8 * par
            eo = 1024 * par
            xa = xoth[par]
            src3 = x_own3 if own else x_oth3
            lda = st("sync", lambda s, ti=ti, xa=xa, src3=src3: s.dma_start(out=xa[:], in_=src3[ti]),
                     waits=[("v", x_hist[par][0]), ("a", x_hist[par][1])], inc="ldma")
            a1 = st("scalar", lambda s, xa=xa, sb=sb: s.activation(
                act_junk[:], xa[:], AF.Identity, accum_out=stt_sc[:, sb:sb + 1]),
                waits=[("ldma", lda), ("v", stats_hist[par])], inc="a")
            a1 = st("scalar", lambda s, xa=xa, sb=sb: s.activation(
                act_junk[:], xa[:], AF.Square, accum_out=stt_sc[:, sb + 1:sb + 2]), inc="a")
            v1 = st("vector", lambda v, sb=sb: v.tensor_single_scalar(
                stt_sc[:, sb + 2:sb + 3], stt_sc[:, sb:sb + 1], 1.0 / H, ALU.mult), waits=[("a", a1)], inc="v")
            v1 = st("vector", lambda v, sb=sb: v.tensor_single_scalar(
                stt_sc[:, sb + 7:sb + 8], stt_sc[:, sb + 1:sb + 2], 1.0 / H, ALU.mult), inc="v")
            v1 = st("vector", lambda v, sb=sb: v.tensor_mul(
                stt_sc[:, sb + 4:sb + 5], stt_sc[:, sb + 2:sb + 3], stt_sc[:, sb + 2:sb + 3]), inc="v")
            SPV()
            v1 = st("vector", lambda v, sb=sb: v.tensor_sub(
                stt_sc[:, sb + 3:sb + 4], stt_sc[:, sb + 7:sb + 8], stt_sc[:, sb + 4:sb + 5]), inc="v")
            stats_hist[par] = v1
            a2 = st("scalar", lambda s, sb=sb: s.activation(
                stt_sc[:, sb + 5:sb + 6], stt_sc[:, sb + 3:sb + 4], AF.Ln, bias=cst[:, 0:1]),
                waits=[("v", v1)], inc="a")
            SPA()
            a2 = st("scalar", lambda s, sb=sb: s.activation(
                stt_sc[:, sb + 6:sb + 7], stt_sc[:, sb + 5:sb + 6], AF.Exp, scale=cst[:, 2:3]), inc="a")
            if own:
                v1 = st("vector", lambda v, xa=xa, ti=ti, sb=sb: v.tensor_scalar(
                    xn_f32[:, H * ti:H * (ti + 1)], xa[:], stt_sc[:, sb + 2:sb + 3], stt_sc[:, sb + 6:sb + 7],
                    ALU.subtract, ALU.mult), waits=[("a", a2)], inc="v")
                v1 = st("vector", lambda v, ti=ti, par=par: v.tensor_copy(
                    xnb_scr2[par][:], xn_f32[:, H * ti:H * (ti + 1)]),
                    waits=[("trdma", tr_hist[par])], inc="v")
                tdst = xnT3[:, :, 128 * ti:128 * (ti + 1)]
            else:
                v1 = st("vector", lambda v, xa=xa, par=par, sb=sb: v.tensor_scalar(
                    xnb_scr2[par][:], xa[:], stt_sc[:, sb + 2:sb + 3], stt_sc[:, sb + 6:sb + 7],
                    ALU.subtract, ALU.mult), waits=[("a", a2), ("trdma", tr_hist[par])], inc="v")
                tdst = xnoT[par][:].rearrange("p (c l) -> p c l", c=8)
            x_hist[par] = (v1, a1)
            tr1 = st("scalar", lambda s, tdst=tdst, par=par: s.dma_start(
                out=tdst, in_=xnb_scr2[par][:], transpose=True),
                waits=[("v", v1), ("pe", xnoT_pe[par])], inc="trdma")
            tr_hist[par] = tr1
            xsrcT = xnT3[:, :, 128 * ti:128 * (ti + 1)] if own else xnoT[par][:].rearrange("p (c l) -> p c l", c=8)

            pe0 = 0
            w0 = [("trdma", tr1), ("gdma", ldw1), ("v", kv_evac["k"])]
            for c in range(8):
                for half in (0, 1):
                    pe0 = st("tensor", lambda t, c=c, half=half, xsrcT=xsrcT: t.matmul(
                        ps_k[:, 512 * half:512 * (half + 1)], xsrcT[:, c, :],
                        wkv[:, 1024 * c + 512 * half:1024 * c + 512 * (half + 1)],
                        start=(c == 0), stop=False), waits=w0, inc="pe")
                    w0 = []
            pe0 = st("tensor", lambda t: t.matmul(ps_k[:, 0:512], ones1[0:1, 0:128], brows[0:1, H:H + 512],
                                                  start=False, stop=True), inc="pe")
            pe_k = st("tensor", lambda t: t.matmul(ps_k[:, 512:1024], ones1[0:1, 0:128], brows[0:1, H + 512:2 * H],
                                                   start=False, stop=True), inc="pe")
            w0 = [("v", kv_evac["v"])]
            for c in range(8):
                for half in (0, 1):
                    pe0 = st("tensor", lambda t, c=c, half=half, xsrcT=xsrcT: t.matmul(
                        ps_v[:, 512 * half:512 * (half + 1)], xsrcT[:, c, :],
                        wkv[:, 8192 + 1024 * c + 512 * half:8192 + 1024 * c + 512 * (half + 1)],
                        start=(c == 0), stop=False), waits=w0, inc="pe")
                    w0 = []
            pe0 = st("tensor", lambda t: t.matmul(ps_v[:, 0:512], ones1[0:1, 0:128], brows[0:1, 2 * H:2 * H + 512],
                                                  start=False, stop=True), inc="pe")
            pe_kv = st("tensor", lambda t: t.matmul(ps_v[:, 512:1024], ones1[0:1, 0:128], brows[0:1, 2 * H + 512:3 * H],
                                                    start=False, stop=True), inc="pe")
            if not own:
                xnoT_pe[par] = pe_kv
            a3 = st("scalar", lambda s, eo=eo: s.activation(escr[:, eo:eo + 1024], ps_k[:], AF.Exp),
                    waits=[("pe", pe_k)], inc="a")
            v2 = st("vector", lambda v, eo=eo: v.tensor_single_scalar(
                escr[:, eo:eo + 1024], escr[:, eo:eo + 1024], 1.0, ALU.min),
                waits=[("a", a3), ("pe", kevac_gate[par])], inc="v")
            v2 = st("vector", lambda v, par=par, eo=eo: v.scalar_tensor_tensor(
                ke_t[par][:], ps_k[:], 0.0, escr[:, eo:eo + 1024], ALU.max, ALU.add), inc="v")
            kv_evac["k"] = v2
            v3 = st("vector", lambda v, par=par: v.tensor_copy(
                ve_t[par][:].rearrange("p (h c) -> p h c", h=NH)[:, :, 0:64],
                ps_v[:].rearrange("p (h c) -> p h c", h=NH)),
                waits=[("pe", pe_kv)], inc="v")
            kv_evac["v"] = v3
            for hh in range(NH):
                pr = hh // 2
                ro = 64 * (hh % 2)
                dst = ps_c if pr < 4 else ps_d
                co = 65 * (pr % 4)
                pkv = st("tensor", lambda t, hh=hh, ro=ro, dst=dst, co=co, par=par, tt=tt: t.matmul(
                    dst[ro:ro + 64, co:co + 65], ke_t[par][:, 64 * hh:64 * hh + 64],
                    ve_t[par][:, 65 * hh:65 * hh + 65],
                    start=(tt == 0), stop=(tt == NTF - 1)),
                    waits=[("v", v3)] if hh == 0 else [], inc="pe")
            kevac_gate[par] = pkv

        v4 = st("vector", lambda v: v.tensor_copy(kv_sb[:, 0:260], ps_c[:, 0:260]),
                waits=[("pe", pkv)], inc="v")
        v4 = st("vector", lambda v: v.tensor_copy(kv_sb[:, 260:520], ps_d[:, 0:260]), inc="v")

        tap(0, lambda: xn_f32[:, 0:1024])
        tap(1, lambda: kv_sb[:, 0:520])
        b.barrier()

        xn_sp3 = dr["xn_sp"].rearrange("(t p) n -> t p n", p=128)
        st("gpsimd", lambda g: g.dma_start(
            out=dr["xn_sp"].rearrange("(t p) n -> p t n", p=128),
            in_=xn_f32[:].rearrange("p (t n) -> p t n", t=NT)), inc="gdma")
        xn_sp_done = b.n["gdma"]
        prev_tr3 = {0: 0, 1: 0}
        for ti in range(NT):
            par = ti % 2
            eo = 1024 * par
            v5 = st("vector", lambda v, ti=ti: v.tensor_reduce(
                amax1[:, ti:ti + 1], xn_f32[:, H * ti:H * (ti + 1)], AX,
                ALU.max, apply_absolute_value=True), inc="v")
            SPV()
            v5 = st("vector", lambda v, ti=ti: v.tensor_single_scalar(
                inv1[:, ti:ti + 1], amax1[:, ti:ti + 1], 1e-5, ALU.max), inc="v")
            SPV()
            v5 = st("vector", lambda v, ti=ti: v.reciprocal(s1c[:, ti:ti + 1], inv1[:, ti:ti + 1]), inc="v")
            SPV()
            v5 = st("vector", lambda v, ti=ti: v.tensor_single_scalar(
                s1c[:, ti:ti + 1], s1c[:, ti:ti + 1], 127.0, ALU.mult), inc="v")
            v5 = st("vector", lambda v, ti=ti: v.tensor_single_scalar(
                inv1[:, ti:ti + 1], inv1[:, ti:ti + 1], 1.0 / 127.0, ALU.mult), inc="v")
            v5 = st("vector", lambda v, ti=ti, eo=eo: v.tensor_scalar(
                escr[:, eo:eo + 1024], xn_f32[:, H * ti:H * (ti + 1)], s1c[:, ti:ti + 1], MAGIC,
                ALU.mult, ALU.add), inc="v")
            v5 = st("vector", lambda v, par=par, eo=eo: v.tensor_single_scalar(
                xnb_scr2[par][:], escr[:, eo:eo + 1024], MAGIC, ALU.subtract),
                waits=[("trdma", prev_tr3[par])], inc="v")
            prev_tr3[par] = st("scalar", lambda s, ti=ti, par=par: s.dma_start(
                out=n1T3[:, :, 128 * ti:128 * (ti + 1)], in_=xnb_scr2[par][:], transpose=True),
                waits=[("v", v5), ("gdma", xn_sp_done)], inc="trdma")
        st("vector", lambda v: v.tensor_scalar(cg_all[:], inv1[:], mcol[:, 1:2], None, ALU.mult), inc="v")
        st("vector", lambda v: v.tensor_scalar(cu_all[:], inv1[:], mcol[:, 2:3], None, ALU.mult), inc="v")

        tap(3, lambda: n1T[:, 0:1024])
        tap(9, lambda: amax1[:, 0:NT])
        tap(10, lambda: s1c[:, 0:NT])
        b.barrier()

        ldq = st("gpsimd", lambda g: g.dma_start(
            out=wq2[:].rearrange("p (c n) -> p c n", c=8),
            in_=dr["wq_t"].rearrange("(c p) n -> p c n", p=128)), inc="gdma")
        v6 = st("vector", lambda v: v.tensor_copy(kv_bf[:], kv_sb[:]), inc="v")
        v6 = st("vector", lambda v: v.memset(dstat[:], 0.0), inc="v")
        for hh in range(NH):
            pr, ro = hh // 2, 64 * (hh % 2)
            cof = 260 * (pr // 4) + 65 * (pr % 4) + 64
            v6 = st("vector", lambda v, hh=hh, ro=ro, cof=cof: v.tensor_single_scalar(
                dstat[ro:ro + 64, hh:hh + 1], kv_sb[ro:ro + 64, cof:cof + 1], EPS_ATTN, ALU.add), inc="v")
        pe1 = 0
        for half in (0, 1):
            pe1 = st("tensor", lambda t, half=half: t.matmul(
                ps_k[:, 512 * half:512 * (half + 1)], ones1[0:1, 0:128],
                brows[0:1, 4 * H + 512 * half:4 * H + 512 * (half + 1)],
                start=True, stop=True), waits=[("v", v6)] if half == 0 else [], inc="pe")
        v6 = st("vector", lambda v: v.tensor_copy(gb_bc[:, 0:H], ps_k[:, 0:H]),
                waits=[("pe", pe1)], inc="v")
        for half in (0, 1):
            pe1 = st("tensor", lambda t, half=half: t.matmul(
                ps_k[:, 512 * half:512 * (half + 1)], ones1[0:1, 0:128],
                brows[0:1, 5 * H + 512 * half:5 * H + 512 * (half + 1)],
                start=True, stop=True), waits=[("v", v6)] if half == 0 else [], inc="pe")
        v6 = st("vector", lambda v: v.tensor_copy(gb_bc[:, H:2 * H], ps_k[:, 0:H]),
                waits=[("pe", pe1)], inc="v")

        prev_ctx_evac = 0
        a_ln_prev = 0
        zpair_free = 0
        denom_ps = lambda sub: (ps_k if sub < 2 else ps_v)
        for c in range(8):
            peq = 0
            w0 = [("gdma", ldq), ("v", prev_ctx_evac if c else v6)]
            for fin in range(8):
                for sub in range(4):
                    psd = ps_c if sub < 2 else ps_d
                    so = 512 * (sub % 2)
                    peq = st("tensor", lambda t, c=c, fin=fin, sub=sub, psd=psd, so=so: t.matmul(
                        psd[:, so:so + 512], wq2[:, 1024 * fin + 128 * c:1024 * fin + 128 * (c + 1)],
                        xnT3[:, fin, 512 * sub:512 * (sub + 1)],
                        start=(fin == 0), stop=False), waits=w0, inc="pe")
                    w0 = []
            for sub in range(4):
                psd = ps_c if sub < 2 else ps_d
                so = 512 * (sub % 2)
                peq = st("tensor", lambda t, c=c, sub=sub, psd=psd, so=so: t.matmul(
                    psd[:, so:so + 512], brows[0:1, 128 * c:128 * (c + 1)], ones1[0:1, 0:512],
                    start=False, stop=True), inc="pe")
            a4 = 0
            for sub in range(4):
                psd = ps_c if sub < 2 else ps_d
                so = 512 * (sub % 2)
                a4 = st("scalar", lambda s, sub=sub, psd=psd, so=so: s.activation(
                    escr[:, 512 * sub:512 * (sub + 1)], psd[:, so:so + 512], AF.Exp),
                    waits=[("pe", peq)] if sub == 0 else [], inc="a")
            v7 = st("vector", lambda v: v.tensor_single_scalar(
                escr[:, 0:2048], escr[:, 0:2048], 1.0, ALU.min), waits=[("a", a4)], inc="v")
            for sub in range(4):
                psd = ps_c if sub < 2 else ps_d
                so = 512 * (sub % 2)
                v7 = st("vector", lambda v, sub=sub, psd=psd, so=so: v.scalar_tensor_tensor(
                    qe_c[:, 512 * sub:512 * (sub + 1)], psd[:, so:so + 512], 0.0,
                    escr[:, 512 * sub:512 * (sub + 1)], ALU.max, ALU.add), inc="v")
            ped = 0
            for sub in range(4):
                dps = denom_ps(sub)
                so = 512 * (sub % 2)
                for half in (0, 1):
                    ped = st("tensor", lambda t, c=c, sub=sub, dps=dps, so=so, half=half: t.matmul(
                        dps[64 * half:64 * half + 64, so:so + 512],
                        dstat[:, 2 * c + half:2 * c + half + 1].broadcast_to([128, 64]),
                        qe_c[:, 512 * sub:512 * (sub + 1)], start=True, stop=True),
                        waits=[("v", v7), ("a", a_ln_prev)] if sub == 0 and half == 0 else [],
                        inc="pe")
            a_ln = 0
            for sub in range(4):
                dps = denom_ps(sub)
                so = 512 * (sub % 2)
                a_ln = st("scalar", lambda s, sub=sub, dps=dps, so=so: s.activation(
                    zl_buf[:, 512 * sub:512 * (sub + 1)], dps[:, so:so + 512], AF.Ln),
                    waits=[("pe", ped)] if sub == 0 else [], inc="a")
            a_ln_prev = a_ln
            a_z = st("scalar", lambda s: s.activation(
                zpair[:], zl_buf[:], AF.Exp, scale=cst[:, 3:4]),
                waits=[("v", zpair_free)], inc="a")
            pectx = 0
            cof = 260 * (c // 4) + 65 * (c % 4)
            for half in (0, 1):
                for sub in range(4):
                    psd = ps_c if sub < 2 else ps_d
                    so = 512 * (sub % 2)
                    pectx = st("tensor", lambda t, half=half, cof=cof, sub=sub, psd=psd, so=so: t.matmul(
                        psd[64 * half:64 * half + 64, so:so + 512],
                        kv_bf[64 * half:64 * half + 64, cof:cof + 64],
                        qe_c[64 * half:64 * half + 64, 512 * sub:512 * (sub + 1)],
                        start=True, stop=True), inc="pe")
            v8 = 0
            for sub in range(4):
                psd = ps_c if sub < 2 else ps_d
                so = 512 * (sub % 2)
                v8 = st("vector", lambda v, c=c, sub=sub, psd=psd, so=so: v.scalar_tensor_tensor(
                    ctxT3[:, c, 512 * sub:512 * (sub + 1)], psd[:, so:so + 512], 1.0,
                    zpair[:, 512 * sub:512 * (sub + 1)], ALU.mult, ALU.mult),
                    waits=[("pe", pectx), ("a", a_z)] if sub == 0 else [], inc="v")
            prev_ctx_evac = v8
            zpair_free = v8

        prev_zmul = prev_ctx_evac

        ldd = st("gpsimd", lambda g: g.dma_start(
            out=wq2[:].rearrange("p (c n) -> p c n", c=8),
            in_=dr["wd_t"].rearrange("(c p) n -> p c n", p=128)),
            waits=[("pe", b.n["pe"])], inc="gdma")

        prev_s_sp = {0: 0, 1: 0}
        hs_gate = {0: 0, 1: 0}
        trb_gate = {0: 0, 1: 0}
        prev_ln2_v = {0: 0, 1: 0}
        for ti in range(NT):
            par = ti % 2
            sb = 8 * par
            psd_t = ps_k if par == 0 else ps_v
            trb = st("sync", lambda s, ti=ti, par=par: s.dma_start(
                out=xnb_back[par][:], in_=xn_sp3[ti]),
                waits=[("v", trb_gate[par]), ("gdma", xn_sp_done)], inc="ldma")
            ped2 = 0
            w0 = [("gdma", ldd), ("v", prev_zmul), ("v", hs_gate[par])]
            for c in range(8):
                for half in (0, 1):
                    ped2 = st("tensor", lambda t, ti=ti, c=c, half=half, psd_t=psd_t: t.matmul(
                        psd_t[:, 512 * half:512 * (half + 1)],
                        ctxT3[:, c, 128 * ti:128 * (ti + 1)],
                        wq2[:, 1024 * c + 512 * half:1024 * c + 512 * (half + 1)],
                        start=(c == 0), stop=False), waits=w0, inc="pe")
                    w0 = []
            for half in (0, 1):
                ped2 = st("tensor", lambda t, half=half, psd_t=psd_t: t.matmul(
                    psd_t[:, 512 * half:512 * (half + 1)], ones1[0:1, 0:128],
                    brows[0:1, 3 * H + 512 * half:3 * H + 512 * (half + 1)],
                    start=False, stop=True), inc="pe")
            v10 = st("vector", lambda v, par=par, psd_t=psd_t: v.scalar_tensor_tensor(
                hs_scr2[par][:], psd_t[:], 1.0, xnb_back[par][:], ALU.mult, ALU.add),
                waits=[("pe", ped2), ("ldma", trb)], inc="v")
            hs_gate[par] = v10
            if DEBUG_NAN:
                st("gpsimd", lambda g, ti=ti, par=par: g.dma_start(
                    out=dr["dbg_h"].rearrange("(t p) n -> t p n", p=128)[ti], in_=hs_scr2[par][:]),
                    waits=[("v", v10)], inc="gdma")
                st("gpsimd", lambda g, ti=ti, par=par: g.dma_start(
                    out=dr["dbg_x"].rearrange("(t p) n -> t p n", p=128)[ti], in_=xnb_back[par][:]),
                    waits=[("ldma", trb)], inc="gdma")
            trb_gate[par] = v10
            a6 = st("scalar", lambda s, par=par, sb=sb: s.activation(
                act_junk[:], hs_scr2[par][:], AF.Identity, accum_out=stt_sc[:, sb:sb + 1]),
                waits=[("v", v10), ("v", prev_ln2_v[par])], inc="a")
            a6 = st("scalar", lambda s, par=par, sb=sb: s.activation(
                act_junk[:], hs_scr2[par][:], AF.Square, accum_out=stt_sc[:, sb + 1:sb + 2]), inc="a")
            v10 = st("vector", lambda v, sb=sb: v.tensor_single_scalar(
                stt_sc[:, sb + 2:sb + 3], stt_sc[:, sb:sb + 1], 1.0 / H, ALU.mult), waits=[("a", a6)], inc="v")
            v10 = st("vector", lambda v, sb=sb: v.tensor_single_scalar(
                stt_sc[:, sb + 7:sb + 8], stt_sc[:, sb + 1:sb + 2], 1.0 / H, ALU.mult), inc="v")
            v10 = st("vector", lambda v, sb=sb: v.tensor_mul(
                stt_sc[:, sb + 4:sb + 5], stt_sc[:, sb + 2:sb + 3], stt_sc[:, sb + 2:sb + 3]), inc="v")
            SPV()
            v10 = st("vector", lambda v, sb=sb: v.tensor_sub(
                stt_sc[:, sb + 3:sb + 4], stt_sc[:, sb + 7:sb + 8], stt_sc[:, sb + 4:sb + 5]), inc="v")
            prev_ln2_v[par] = v10
            a6 = st("scalar", lambda s, sb=sb: s.activation(
                stt_sc[:, sb + 5:sb + 6], stt_sc[:, sb + 3:sb + 4], AF.Ln, bias=cst[:, 1:2]),
                waits=[("v", v10)], inc="a")
            SPA()
            a6 = st("scalar", lambda s, sb=sb: s.activation(
                stt_sc[:, sb + 6:sb + 7], stt_sc[:, sb + 5:sb + 6], AF.Exp, scale=cst[:, 2:3]), inc="a")
            if DEBUG_NAN:
                st("gpsimd", lambda g, ti=ti: g.dma_start(out=dr["dbg_t"][ti], in_=stt_sc[:]),
                    waits=[("a", a6)], inc="gdma")
            v11 = st("vector", lambda v, par=par, sb=sb: v.tensor_scalar(
                hs_scr2[par][:], hs_scr2[par][:], stt_sc[:, sb + 2:sb + 3], stt_sc[:, sb + 6:sb + 7],
                ALU.subtract, ALU.mult),
                waits=[("a", a6)], inc="v")
            v11 = st("vector", lambda v, par=par: v.tensor_mul(
                hs_scr2[par][:], hs_scr2[par][:], gb_bc[:, 0:H]), inc="v")
            v11 = st("vector", lambda v, par=par: v.tensor_add(
                s_tile2[par][:], hs_scr2[par][:], gb_bc[:, H:2 * H]),
                waits=[("gdma", prev_s_sp[par])], inc="v")
            prev_s_sp[par] = st("gpsimd", lambda g, ti=ti, par=par: g.dma_start(
                out=s_sp3[ti], in_=s_tile2[par][:]), waits=[("v", v11)], inc="gdma")
            if DEBUG_NAN:
                prev_s_sp[par] = st("gpsimd", lambda g, ti=ti, par=par: g.dma_start(
                    out=dr["dbg_s"].rearrange("(t p) n -> t p n", p=128)[ti], in_=s_tile2[par][:]),
                    waits=[("v", v11)], inc="gdma")

        tap(4, lambda: qe_c[:, 0:1024])
        tap(5, lambda: zl_buf[:, 0:1024])
        tap(6, lambda: ctxT[:, 0:1024])
        tap(7, lambda: hs_scr2[0][:, 0:1024])
        tap(11, lambda: dstat[:, 0:128])
        if DEBUG_NAN:
            st("gpsimd", lambda g: g.dma_start(out=dr["dbg_n"], in_=n1T[:]),
               waits=[("trdma", b.n["trdma"])], inc="gdma")
            st("gpsimd", lambda g: g.dma_start(out=dr["dbg_c"], in_=ctxT[:]),
               waits=[("v", b.n["v"])], inc="gdma")
            st("gpsimd", lambda g: g.dma_start(out=dr["dbg_i"][:, 0:16], in_=inv1[:]),
               waits=[("v", b.n["v"])], inc="gdma")
            st("gpsimd", lambda g: g.dma_start(out=dr["dbg_i"][:, 16:32], in_=cg_all[:]), inc="gdma")
            st("gpsimd", lambda g: g.dma_start(out=dr["dbg_i"][:, 32:48], in_=cu_all[:]), inc="gdma")
            st("gpsimd", lambda g: g.dma_start(out=dr["dbg_i"][:, 48:64], in_=mcol[:]), inc="gdma")
        b.barrier()

        tern_g3 = dr["tern_g"].rearrange("(c p) n -> p c n", p=128)
        tern_u3 = dr["tern_u"].rearrange("(c p) n -> p c n", p=128)
        tern_d4 = dr["tern_d"].rearrange("(c p) n -> p c n", p=128)
        wgu_used = {0: 0, 1: 0}
        down_used = {0: 0, 1: 0}
        evac_gate = {"a": 0, "v": 0}
        a8_hist = {0: 0, 1: 0}
        v14_hist = {0: 0, 1: 0}
        st_hist = {0: 0, 1: 0}
        ped3 = 0
        for g in range(NGRP):
            for iq in range(4):
                ldg_g = st("sync", lambda s, iq=iq: s.dma_start(
                    out=wgu[:, 0:8 * 1024].rearrange("p (c n) -> p c n", c=8),
                    in_=tern_g3[:, :, 1024 * iq:1024 * (iq + 1)]),
                    waits=[("pe", wgu_used[0])], inc="ldma")
                ldg_u = st("gpsimd", lambda s, iq=iq: s.dma_start(
                    out=wgu[:, 8 * 1024:16 * 1024].rearrange("p (c n) -> p c n", c=8),
                    in_=tern_u3[:, :, 1024 * iq:1024 * (iq + 1)]),
                    waits=[("pe", wgu_used[0])], inc="gdma")
                for tl in range(GT):
                    ti = g * GT + tl
                    peg = 0
                    w0 = [("ldma", ldg_g), ("gdma", ldg_u),
                          ("a", evac_gate["a"]), ("v", evac_gate["v"])]
                    for c in range(8):
                        for (psd, wof) in ((ps_k, 0), (ps_v, 8192)):
                            for half in (0, 1):
                                peg = st("tensor", lambda t, ti=ti, c=c, psd=psd, wof=wof, half=half: t.matmul(
                                    psd[:, 512 * half:512 * (half + 1)],
                                    n1T3[:, c, 128 * ti:128 * (ti + 1)],
                                    wgu[:, wof + 1024 * c + 512 * half:wof + 1024 * c + 512 * (half + 1)],
                                    start=(c == 0), stop=(c == 7)), waits=w0, inc="pe")
                                w0 = []
                    if tl == GT - 1:
                        wgu_used[0] = peg
                    a7 = st("scalar", lambda s, ti=ti: s.activation(
                        sg_scr[:], ps_k[:], AF.Silu, scale=cg_all[:, ti:ti + 1]),
                        waits=[("pe", peg), ("v", evac_gate["v"])], inc="a")
                    v12 = st("vector", lambda v, ti=ti, tl=tl, iq=iq: v.scalar_tensor_tensor(
                        t3[:, 4096 * tl + 1024 * iq:4096 * tl + 1024 * (iq + 1)],
                        ps_v[:], cu_all[:, ti:ti + 1], sg_scr[:], ALU.mult, ALU.mult),
                        waits=[("a", a7)], inc="v")
                    evac_gate = {"a": a7, "v": v12}
            if DEBUG_NAN and g == 0:
                st("gpsimd", lambda g2: g2.dma_start(out=dr["dbg_t3"], in_=t3[:]),
                   waits=[("v", b.n["v"])], inc="gdma")
            for hp in range(GT // 2):
                tr4s = {}
                for par in (0, 1):
                    tl = 2 * hp + par
                    ti = g * GT + tl
                    pc = 4 * par
                    v13 = st("vector", lambda v, tl=tl, pc=pc: v.tensor_reduce(
                        p4c[:, pc:pc + 1], t3[:, 4096 * tl:4096 * (tl + 1)], AX,
                        ALU.max, apply_absolute_value=True),
                        waits=[("a", a8_hist[par])], inc="v")
                    SPV()
                    v13 = st("vector", lambda v, pc=pc: v.tensor_single_scalar(
                        p4c[:, pc + 1:pc + 2], p4c[:, pc:pc + 1], 1e-5, ALU.max), inc="v")
                    SPV()
                    v13 = st("vector", lambda v, pc=pc: v.reciprocal(
                        p4c[:, pc + 2:pc + 3], p4c[:, pc + 1:pc + 2]), inc="v")
                    SPV()
                    v13 = st("vector", lambda v, pc=pc: v.tensor_single_scalar(
                        p4c[:, pc + 2:pc + 3], p4c[:, pc + 2:pc + 3], 127.0, ALU.mult), inc="v")
                    v13 = st("vector", lambda v, pc=pc: v.tensor_scalar(
                        p4c[:, pc + 3:pc + 4], p4c[:, pc + 1:pc + 2], mcol[:, 3:4], 1.0 / 127.0,
                        ALU.mult, ALU.mult), inc="v")
                    SPV()
                    trw = b.n["trdma"]
                    for qs in range(4):
                        v13 = st("vector", lambda v, tl=tl, pc=pc, qs=qs: v.tensor_scalar(
                            y2_scr[:], t3[:, 4096 * tl + 1024 * qs:4096 * tl + 1024 * (qs + 1)],
                            p4c[:, pc + 2:pc + 3], MAGIC, ALU.mult, ALU.add),
                            waits=[("trdma", trw)] if qs == 0 else [], inc="v")
                        v13 = st("vector", lambda v, qs=qs: v.tensor_single_scalar(
                            n2tok[:, 1024 * qs:1024 * (qs + 1)], y2_scr[:], MAGIC, ALU.subtract), inc="v")
                    tr4s[par] = st("scalar", lambda s, par=par: s.dma_start(
                        out=n2T[par][:].rearrange("p (c l) -> p c l", c=32),
                        in_=n2tok[:], transpose=True),
                        waits=[("v", v13), ("pe", down_used[par])], inc="trdma")
                ped_hist = {0: ped3, 1: ped3}
                for cpair in range(16):
                    cpar = cpair % 2
                    ring, rsem = ("sync", "ldma") if cpar == 0 else ("gpsimd", "gdma")
                    ldd2 = st(ring, lambda s, cpair=cpair, cpar=cpar: s.dma_start(
                        out=tdc[cpar][:].rearrange("p (c n) -> p c n", c=2),
                        in_=tern_d4[:, 2 * cpair:2 * cpair + 2, :]),
                        waits=[("pe", ped_hist[cpar])], inc=rsem)
                    for ci in (0, 1):
                        c = 2 * cpair + ci
                        for par in (0, 1):
                            psd = ps_c if par == 0 else ps_d
                            for half in (0, 1):
                                ped3 = st("tensor", lambda t, c=c, ci=ci, cpar=cpar, par=par, half=half, psd=psd: t.matmul(
                                    psd[:, 512 * half:512 * (half + 1)],
                                    n2T[par][:, 128 * c:128 * (c + 1)],
                                    tdc[cpar][:, 1024 * ci + 512 * half:1024 * ci + 512 * (half + 1)],
                                    start=(c == 0), stop=(c == 31)),
                                    waits=([("trdma", tr4s[par]), (rsem, ldd2),
                                            ("a", a8_hist[0] if c == 0 else 0),
                                            ("a", a8_hist[1] if c == 0 else 0)]
                                           if half == 0 and ci == 0 else []),
                                    inc="pe")
                    ped_hist[cpar] = ped3
                for par in (0, 1):
                    down_used[par] = ped3
                for par in (0, 1):
                    tl = 2 * hp + par
                    ti = g * GT + tl
                    pc = 4 * par
                    psd = ps_c if par == 0 else ps_d
                    ldx = st("sync", lambda s, ti=ti, par=par: s.dma_start(out=xnb_back[par][:], in_=x_own3[ti]),
                             waits=[("v", v14_hist[par])], inc="ldma")
                    lds = st("gpsimd", lambda g2, ti=ti, par=par: g2.dma_start(out=s_tile2[par][:], in_=s_sp3[ti]),
                             waits=[("v", v14_hist[par])], inc="gdma")
                    a8 = st("scalar", lambda s, par=par, pc=pc, psd=psd: s.activation(
                        act_junk[:], psd[:], AF.Silu, scale=p4c[:, pc + 3:pc + 4]),
                        waits=[("pe", ped3), ("v", v14_hist[0]), ("v", v14_hist[1])], inc="a")
                    a8_hist[par] = a8
                    if DEBUG_NAN:
                        st("gpsimd", lambda g2, ti=ti: g2.dma_start(
                            out=dr["dbg_g"].rearrange("(t p) n -> t p n", p=128)[ti], in_=act_junk[:]),
                            waits=[("a", a8)], inc="gdma")
                    v14 = st("vector", lambda v, par=par: v.tensor_mul(act_junk[:], act_junk[:], s_tile2[par][:]),
                             waits=[("a", a8), ("gdma", lds)], inc="v")
                    v14 = st("vector", lambda v, par=par: v.tensor_add(hs_scr2[par][:], act_junk[:], xnb_back[par][:]),
                             waits=[("ldma", ldx), ("stdma", st_hist[par])], inc="v")
                    v14_hist[par] = v14
                    st_hist[par] = st("sync", lambda s, ti=ti, par=par: s.dma_start(
                        out=out3[ti], in_=hs_scr2[par][:]), waits=[("v", v14)], inc="stdma")

        tap(8, lambda: t3[:, 0:1024])
        b.steps.append(Step("sync", None, [("stdma", b.n["stdma"])], None))

        emit(b, nc)
    return nc


def emit(b, nc):
    sem_handles = {}
    with ExitStack() as ctx:
        for s in SEMS:
            sem_handles[s] = ctx.enter_context(nc.semaphore(s))
        with nc.Block() as block:
            for name in ENGINES:
                section = getattr(block, name)

                def body(eng, name=name):
                    for stp in b.steps:
                        if stp.engine != name:
                            continue
                        for (s, val) in stp.waits:
                            eng.wait_ge(sem_handles[s], val)
                        if stp.emit is None:
                            continue
                        inst = stp.emit(eng)
                        if stp.incs is not None and inst is not None:
                            inst.then_inc(sem_handles[stp.incs[0]], stp.incs[1])

                section(body)


def kernel(**inputs):
    x = np.ascontiguousarray(np.asarray(inputs["L0_attn_input_tensor"], dtype=np.float32))
    row = lambda a: np.ascontiguousarray(np.asarray(a, np.float32)).reshape(1, -1)
    tr = lambda a: np.ascontiguousarray(np.asarray(a, np.float32).T)
    shared = {
        "wq_t": tr(inputs["L0_attn_Wq"]), "wk_t": tr(inputs["L0_attn_Wk"]),
        "wv_t": tr(inputs["L0_attn_Wv"]), "wd_t": tr(inputs["L0_attn_Wd"]),
        "wg_t": tr(inputs["L0_ffn_w_gate"]), "wu_t": tr(inputs["L0_ffn_w_up"]),
        "wdn_t": tr(inputs["L0_ffn_w_down"]),
        "bq": row(inputs["L0_attn_bq"]), "bk": row(inputs["L0_attn_bk"]),
        "bv": row(inputs["L0_attn_bv"]), "bd": row(inputs["L0_attn_bd"]),
        "gamma": row(inputs["L0_attn_gamma"]), "beta": row(inputs["L0_attn_beta"]),
    }
    in_maps = []
    for c in range(8):
        n, h = c // 2, c % 2
        m = dict(shared)
        m["x_own"] = np.ascontiguousarray(x[n, h * TOWN:(h + 1) * TOWN])
        m["x_oth"] = np.ascontiguousarray(x[n, (1 - h) * TOWN:(2 - h) * TOWN])
        in_maps.append(m)
    nc = build_core_program()
    res = run_bass_kernel_spmd(nc, in_maps, core_ids=list(range(8)))
    out = np.empty((4, 4096, 1024), np.float32)
    for c in range(8):
        n, h = c // 2, c % 2
        out[n, h * TOWN:(h + 1) * TOWN] = res.results[c]["out"]
    return out


# version 04
# speedup vs baseline: 1.0934x; 1.0125x over previous
import numpy as np
from contextlib import ExitStack

import concourse.bass as bass
import concourse.mybir as mybir
from concourse.bass_utils import run_bass_kernel_spmd

F32 = mybir.dt.float32
BF16 = mybir.dt.bfloat16
AF = mybir.ActivationFunctionType
ALU = mybir.AluOpType
AX = mybir.AxisListType.X

MAGIC = 12582912.0
H = 1024
NH = 16
INNER = 4096
TOWN = 2048
NT = TOWN // 128
NTF = 32
EPS_LN1 = 1e-5
EPS_ATTN = 1e-6
EPS_LN2 = 1e-12
NGRP = 4
GT = NT // NGRP

DEBUG_TAPS = False
DEBUG_NAN = False
SBUF_BASE = 16512
SBUF_TOP = 229344
ENGINES = ("sync", "gpsimd", "scalar", "vector", "tensor")
SEMS = ("ldma", "stdma", "trdma", "gdma", "tdma", "pe", "v", "a")


class Step:
    __slots__ = ("engine", "emit", "waits", "incs")

    def __init__(self, engine, emit, waits, incs):
        self.engine = engine
        self.emit = emit
        self.waits = waits
        self.incs = incs


class B:

    def __init__(self):
        self.steps = []
        self.n = {s: 0 for s in SEMS}

    def step(self, engine, emit, waits=(), inc=None, inc_amt=None):
        w = [(s, v) for (s, v) in waits if v and v > 0]
        incs = None
        after = None
        if inc is not None:
            if inc_amt is None:
                inc_amt = 16 if inc.endswith("dma") else 1
            self.n[inc] += inc_amt
            after = self.n[inc]
            incs = (inc, inc_amt)
        self.steps.append(Step(engine, emit, w, incs))
        return after

    def barrier(self):
        snap = [(s, self.n[s]) for s in SEMS if self.n[s] > 0]
        for e in ENGINES:
            self.steps.append(Step(e, None, list(snap), None))


class Alloc:

    def __init__(self, nc):
        self.nc = nc
        self.allocs = []

    def __call__(self, name, shape, dtype, phases, align=128):
        size = int(np.prod(shape[1:])) * mybir.dt.size(dtype)
        phases = set(phases)
        spans = sorted(
            (off, off + sz) for (_, off, sz, ph) in self.allocs if ph & phases
        )
        pos = SBUF_BASE
        for (s, e) in spans:
            if pos + size <= s:
                break
            pos = max(pos, (e + align - 1) // align * align)
        pos = (pos + align - 1) // align * align
        if pos + size > SBUF_TOP:
            raise RuntimeError(f"SBUF overflow: {name} needs {size}B at {pos}")
        self.allocs.append((name, pos, size, phases))
        return self.nc.alloc_sbuf_tensor_at(name, list(shape), dtype, offset=pos)


def build_core_program():
    nc = bass.Bass("TRN2", target_bir_lowering=False, debug=False)
    dr = {}
    dr["x_own"] = nc.dram_tensor("x_own", [TOWN, H], F32, kind="ExternalInput").ap()
    dr["x_oth"] = nc.dram_tensor("x_oth", [TOWN, H], F32, kind="ExternalInput").ap()
    for w in ("wq_t", "wk_t", "wv_t", "wd_t"):
        dr[w] = nc.dram_tensor(w, [H, H], F32, kind="ExternalInput").ap()
    dr["wg_t"] = nc.dram_tensor("wg_t", [H, INNER], F32, kind="ExternalInput").ap()
    dr["wu_t"] = nc.dram_tensor("wu_t", [H, INNER], F32, kind="ExternalInput").ap()
    dr["wdn_t"] = nc.dram_tensor("wdn_t", [INNER, H], F32, kind="ExternalInput").ap()
    for r in ("bq", "bk", "bv", "bd", "gamma", "beta"):
        dr[r] = nc.dram_tensor(r, [1, H], F32, kind="ExternalInput").ap()
    dr["out"] = nc.dram_tensor("out", [TOWN, H], F32, kind="ExternalOutput").ap()
    dr["tern_g"] = nc.dram_tensor("tern_g", [H, INNER], BF16, kind="Internal").ap()
    dr["tern_u"] = nc.dram_tensor("tern_u", [H, INNER], BF16, kind="Internal").ap()
    dr["tern_d"] = nc.dram_tensor("tern_d", [INNER, H], BF16, kind="Internal").ap()
    dr["s_sp"] = nc.dram_tensor("s_sp", [TOWN, H], BF16, kind="Internal").ap()
    dr["xn_sp"] = nc.dram_tensor("xn_sp", [TOWN, H], F32, kind="Internal").ap()
    if DEBUG_TAPS:
        dr["dbg"] = nc.dram_tensor("dbg", [12, 128, 1024], F32, kind="ExternalOutput").ap()
    if DEBUG_NAN:
        dr["dbg_s"] = nc.dram_tensor("dbg_s", [TOWN, H], BF16, kind="ExternalOutput").ap()
        dr["dbg_g"] = nc.dram_tensor("dbg_g", [TOWN, H], F32, kind="ExternalOutput").ap()
        dr["dbg_h"] = nc.dram_tensor("dbg_h", [TOWN, H], F32, kind="ExternalOutput").ap()
        dr["dbg_x"] = nc.dram_tensor("dbg_x", [TOWN, H], F32, kind="ExternalOutput").ap()
        dr["dbg_n"] = nc.dram_tensor("dbg_n", [128, 8 * TOWN], BF16, kind="ExternalOutput").ap()
        dr["dbg_c"] = nc.dram_tensor("dbg_c", [128, 8 * TOWN], BF16, kind="ExternalOutput").ap()
        dr["dbg_i"] = nc.dram_tensor("dbg_i", [128, 80], F32, kind="ExternalOutput").ap()
        dr["dbg_t3"] = nc.dram_tensor("dbg_t3", [128, 4 * 4096], F32, kind="ExternalOutput").ap()
        dr["dbg_cr"] = nc.dram_tensor("dbg_cr", [128, 8 * TOWN], BF16, kind="ExternalOutput").ap()
        dr["dbg_z"] = nc.dram_tensor("dbg_z", [16, 2 * TOWN], F32, kind="ExternalOutput").ap()
        dr["dbg_t"] = nc.dram_tensor("dbg_t", [NT, 128, 8], F32, kind="ExternalOutput").ap()

    b = B()
    al = Alloc(nc)

    ALL = ("W", "P1", "P3", "P2", "P4")
    wbig = al("wbig", [128, 8 * 4096], F32, ("W",))
    xn2 = [al(f"xn2_{i}", [128, H], F32, ("P1",)) for i in range(2)]
    t3 = al("t3", [128, GT * 4096], F32, ("P4",))
    xnT = al("xnT", [128, 8 * TOWN], BF16, ("P1", "P3", "P2"))
    n1T = al("n1T", [128, 8 * TOWN], BF16, ("P1", "P3", "P2", "P4"))
    ctxT = al("ctxT", [128, 8 * TOWN], BF16, ("P2",))
    wgu = al("wgu", [128, 16 * 1024], BF16, ("P4",))
    wkv = al("wkv", [128, 16 * 1024], BF16, ("P1",))
    wq2 = al("wq2", [128, 8 * 1024], BF16, ("P2",))
    xoth = [al(f"xoth{i}", [128, H], F32, ("P1",)) for i in range(2)]
    xnb_scr2 = [al(f"xnb_scr{i}", [128, H], BF16, ("P1", "P3")) for i in range(2)]
    q3scr = al("q3scr", [128, 1024], F32, ("P1",))
    n1b2 = [al(f"n1b{i}", [128, H], BF16, ("P1",)) for i in range(2)]
    xnoT = [al(f"xnoT{i}", [128, 8 * 128], BF16, ("P1",)) for i in range(2)]
    ke_t = [al(f"ke{i}", [128, H], BF16, ("P1",)) for i in range(2)]
    ve_t = [al(f"ve{i}", [128, 65 * NH], BF16, ("P1",)) for i in range(2)]
    escr = al("escr", [128, TOWN], F32, ("P1", "P3", "P2"))
    act_junk = al("act_junk", [128, H], F32, ("P1", "P2", "P4"))
    qe_c = al("qe_c", [128, TOWN], BF16, ("P2",))
    zl_buf = al("zl_buf", [128, TOWN], F32, ("P2",))
    zpair = al("zpair", [128, TOWN], F32, ("P2",))
    kv_sb = al("kv_sb", [128, 520], F32, ("P1", "P2"))
    kv_bf = al("kv_bf", [128, 520], BF16, ("P2",))
    dstat = al("dstat", [128, 16], BF16, ("P2",))
    gb_bc = al("gb_bc", [128, 2 * H], F32, ("P2",))
    hs_scr2 = [al(f"hs_scr{i}", [128, H], F32, ("P2", "P4")) for i in range(2)]
    xnb_back = [al(f"xnbk{i}", [128, H], F32, ("P2", "P4")) for i in range(2)]
    s_tile2 = [al(f"s_tile{i}", [128, H], BF16, ("P2", "P4")) for i in range(2)]
    n2tok = al("n2tok", [128, 4096], BF16, ("P4",))
    n2T = [al(f"n2T{i}", [128, 32 * 128], BF16, ("P4",)) for i in range(2)]
    tdc = [al(f"tdc{i}", [128, 2 * H], BF16, ("P4",)) for i in range(2)]
    y2_scr = al("y2_scr", [128, 1024], F32, ("P4",))
    sg_scr = al("sg_scr", [128, 1024], F32, ("P4",))
    tern_scr = al("tern_scr", [128, 4096], BF16, ("W",))
    g1_scr = al("g1_scr", [128, 4096], BF16, ("W",))
    g2_scr = al("g2_scr", [128, 4096], BF16, ("W",))
    ones1 = al("ones1", [1, 512], BF16, ALL)
    ones128 = al("ones128", [128, 128], BF16, ("W",))
    brows = al("brows", [1, 6 * H], BF16, ("W", "P1", "P3", "P2"))
    mcol = al("mcol", [128, 16], F32, ALL)
    amax1 = al("amax1", [128, NT], F32, ("P1", "P3", "P2", "P4"))
    inv1 = al("inv1", [128, NT], F32, ("P1", "P3", "P2", "P4"))
    s1c = al("s1c", [128, NT], F32, ("P1", "P3", "P2", "P4"))
    cg_all = al("cg_all", [128, NT], F32, ("P3", "P2", "P4"))
    cu_all = al("cu_all", [128, NT], F32, ("P3", "P2", "P4"))
    stt_sc = al("stt_sc", [128, 16], F32, ALL)
    cst = al("cst", [128, 4], F32, ALL)
    spv_t = al("spv", [128, 64], F32, ALL)
    spa_t = al("spa", [128, 64], F32, ALL)
    p4c = al("p4c", [128, 8], F32, ("P4",))

    with ExitStack() as ps_ctx:
        PE_ = ps_ctx.enter_context
        ps_k = PE_(nc.psum_tensor("ps_k", [128, 1024], F32))
        ps_v = PE_(nc.psum_tensor("ps_v", [128, 1024], F32))
        ps_c = PE_(nc.psum_tensor("ps_c", [128, 1024], F32))
        ps_d = PE_(nc.psum_tensor("ps_d", [128, 1024], F32))

        st = b.step
        x_own3 = dr["x_own"].rearrange("(t p) n -> t p n", p=128)
        x_oth3 = dr["x_oth"].rearrange("(t p) n -> t p n", p=128)
        out3 = dr["out"].rearrange("(t p) n -> t p n", p=128)
        s_sp3 = dr["s_sp"].rearrange("(t p) n -> t p n", p=128)
        xn_sp3 = dr["xn_sp"].rearrange("(t p) n -> t p n", p=128)
        xnT3 = xnT[:].rearrange("p (c l) -> p c l", c=8)
        n1T3 = n1T[:].rearrange("p (c l) -> p c l", c=8)
        ctxT3 = ctxT[:].rearrange("p (c l) -> p c l", c=8)

        dbg_scr = al("dbgscr", [128, 1024], F32, ALL) if DEBUG_TAPS else None

        def tap(i, get_ap, eng="vector"):
            if not DEBUG_TAPS:
                return
            waits = [(s, b.n[s]) for s in SEMS if b.n[s] > 0]
            st("vector", lambda v: v.tensor_copy(
                dbg_scr[0:get_ap().shape[0], 0:get_ap().shape[-1]], get_ap()),
               waits=waits, inc="v")
            st("sync", lambda s, i=i: s.dma_start(out=dr["dbg"][i], in_=dbg_scr[:]),
               waits=[("v", b.n["v"])], inc="stdma")

        def SPV():
            st("vector", lambda v: v.tensor_copy(spv_t[:], spv_t[:]), inc="v")

        def SPA():
            st("scalar", lambda s: s.copy(spa_t[:], spa_t[:]), inc="a")
        st("vector", lambda v: v.memset(spv_t[:], 0.0), inc="v")
        st("scalar", lambda s: s.memzero(spa_t[:]), inc="a")
        st("vector", lambda v: v.memset(ones1[:], 1.0), inc="v")
        st("vector", lambda v: v.memset(cst[:, 0:1], EPS_LN1), inc="v")
        st("vector", lambda v: v.memset(cst[:, 1:2], EPS_LN2), inc="v")
        st("vector", lambda v: v.memset(cst[:, 2:3], -0.5), inc="v")
        st("vector", lambda v: v.memset(cst[:, 3:4], -1.0), inc="v")
        st("vector", lambda v: v.memset(ones128[:], 1.0), inc="v")
        for i, r in enumerate(("bq", "bk", "bv", "bd", "gamma", "beta")):
            st("gpsimd", lambda g, i=i, r=r: g.dma_start(out=brows[0:1, H * i:H * (i + 1)], in_=dr[r]), inc="gdma")

        specs = [
            (dr["wg_t"], dr["tern_g"], 8, 4096, 1),
            (dr["wu_t"], dr["tern_u"], 8, 4096, 2),
            (dr["wdn_t"], dr["tern_d"], 32, 1024, 3),
        ]
        NW = float(H * INNER)
        v_prev_mat = 0
        g_spill_prev = 0
        for (wsrc, tdst, nch, ccols, slot) in specs:
            wsrc3 = wsrc.rearrange("(c p) n -> p c n", p=128)
            tdst3 = tdst.rearrange("(c p) n -> c p n", p=128)
            ld0 = st("sync", lambda s, wsrc3=wsrc3, nch=nch, ccols=ccols: s.dma_start(
                out=wbig[:, 0:nch * ccols].rearrange("p (c n) -> p c n", c=nch), in_=wsrc3),
                waits=[("v", v_prev_mat)], inc="ldma")
            pe_hist = {0: 0, 1: 0}
            for c in range(8 if ccols == 4096 else 32):
                par = c % 2
                ab = g1_scr if par == 0 else g2_scr
                vd = st("scalar", lambda s, c=c, ab=ab, ccols=ccols: s.activation(
                    ab[:, 0:ccols], wbig[:, c * ccols:(c + 1) * ccols], AF.Abs),
                    waits=[("ldma", ld0), ("pe", pe_hist[par])], inc="a")
                pp = 0
                for gcol in range(ccols // 512):
                    first = (c == 0 and gcol == 0)
                    last = (c == nch - 1 and gcol == ccols // 512 - 1)
                    pp = st("tensor", lambda t, ab=ab, gcol=gcol, first=first, last=last: t.matmul(
                        ps_d[:, 0:512], ones128[:], ab[:, 512 * gcol:512 * (gcol + 1)],
                        start=first, stop=last),
                        waits=[("a", vd)] if gcol == 0 else [], inc="pe")
                pe_hist[par] = pp
            pe_abs = b.n["pe"]
            vv = st("vector", lambda v: v.tensor_reduce(mcol[:, 0:1], ps_d[:, 0:512], AX, ALU.add),
                    waits=[("pe", pe_abs)], inc="v")
            SPV()
            vv = st("vector", lambda v, slot=slot: v.tensor_scalar(
                mcol[:, slot:slot + 1], mcol[:, 0:1], 1.0 / NW, 1e-5, ALU.mult, ALU.max), inc="v")
            SPV()
            vv = st("vector", lambda v, slot=slot: v.tensor_single_scalar(
                mcol[:, 3 + slot:4 + slot], mcol[:, slot:slot + 1], 0.5, ALU.mult), inc="v")
            SPV()
            vv = b.n["v"]
            for c in range(nch):
                v1 = st("vector", lambda v, c=c, ccols=ccols, slot=slot: v.tensor_single_scalar(
                    g1_scr[:, 0:ccols], wbig[:, c * ccols:(c + 1) * ccols],
                    mcol[:, 3 + slot:4 + slot], ALU.is_gt), waits=[("v", vv)], inc="v")
                v1 = st("vector", lambda v, c=c, ccols=ccols, slot=slot: v.tensor_scalar(
                    g2_scr[:, 0:ccols], wbig[:, c * ccols:(c + 1) * ccols],
                    -1.0, mcol[:, 3 + slot:4 + slot], ALU.mult, ALU.is_gt), inc="v")
                v1 = st("vector", lambda v, ccols=ccols: v.tensor_sub(
                    tern_scr[:, 0:ccols], g1_scr[:, 0:ccols], g2_scr[:, 0:ccols]),
                    waits=[("gdma", g_spill_prev)], inc="v")
                g_spill_prev = st("gpsimd", lambda g, c=c, tdst3=tdst3, ccols=ccols: g.dma_start(
                    out=tdst3[c], in_=tern_scr[:, 0:ccols]), waits=[("v", v1)], inc="gdma")
            v_prev_mat = b.n["v"]

        tap(2, lambda: mcol[:].rearrange("p n -> p n"))
        b.barrier()

        ldw1 = st("gpsimd", lambda g: g.dma_start(
            out=wkv[:, 0:8 * 1024].rearrange("p (c n) -> p c n", c=8),
            in_=dr["wk_t"].rearrange("(c p) n -> p c n", p=128)), inc="gdma")
        ldw1 = st("gpsimd", lambda g: g.dma_start(
            out=wkv[:, 8 * 1024:16 * 1024].rearrange("p (c n) -> p c n", c=8),
            in_=dr["wv_t"].rearrange("(c p) n -> p c n", p=128)), inc="gdma")
        for i in range(2):
            st("vector", lambda v, i=i: v.memset(
                ve_t[i][:].rearrange("p (h c) -> p h c", h=NH)[:, :, 64:65], 1.0), inc="v")

        kv_evac = {"k": 0, "v": 0}
        x_hist = {0: (0, 0), 1: (0, 0)}
        kevac_gate = {0: 0, 1: 0}
        tr_hist = {0: 0, 1: 0}
        xnoT_pe = {0: 0, 1: 0}
        n1tr_hist = {0: 0, 1: 0}
        xnsp_hist = {0: 0, 1: 0}
        stats_hist = {0: 0, 1: 0}
        pkv = 0
        for tt in range(NTF):
            own = tt < NT
            ti = tt if own else tt - NT
            par = tt % 2
            sb = 8 * par
            eo = 1024 * par
            xa = xoth[par]
            src3 = x_own3 if own else x_oth3
            lda = st("sync", lambda s, ti=ti, xa=xa, src3=src3: s.dma_start(out=xa[:], in_=src3[ti]),
                     waits=[("v", x_hist[par][0]), ("a", x_hist[par][1])], inc="ldma")
            a1 = st("scalar", lambda s, xa=xa, sb=sb: s.activation(
                act_junk[:], xa[:], AF.Identity, accum_out=stt_sc[:, sb:sb + 1]),
                waits=[("ldma", lda), ("v", stats_hist[par])], inc="a")
            a1 = st("scalar", lambda s, xa=xa, sb=sb: s.activation(
                act_junk[:], xa[:], AF.Square, accum_out=stt_sc[:, sb + 1:sb + 2]), inc="a")
            v1 = st("vector", lambda v, sb=sb: v.tensor_single_scalar(
                stt_sc[:, sb + 2:sb + 3], stt_sc[:, sb:sb + 1], 1.0 / H, ALU.mult), waits=[("a", a1)], inc="v")
            v1 = st("vector", lambda v, sb=sb: v.tensor_single_scalar(
                stt_sc[:, sb + 7:sb + 8], stt_sc[:, sb + 1:sb + 2], 1.0 / H, ALU.mult), inc="v")
            v1 = st("vector", lambda v, sb=sb: v.tensor_mul(
                stt_sc[:, sb + 4:sb + 5], stt_sc[:, sb + 2:sb + 3], stt_sc[:, sb + 2:sb + 3]), inc="v")
            SPV()
            v1 = st("vector", lambda v, sb=sb: v.tensor_sub(
                stt_sc[:, sb + 3:sb + 4], stt_sc[:, sb + 7:sb + 8], stt_sc[:, sb + 4:sb + 5]), inc="v")
            stats_hist[par] = v1
            a2 = st("scalar", lambda s, sb=sb: s.activation(
                stt_sc[:, sb + 5:sb + 6], stt_sc[:, sb + 3:sb + 4], AF.Ln, bias=cst[:, 0:1]),
                waits=[("v", v1)], inc="a")
            SPA()
            a2 = st("scalar", lambda s, sb=sb: s.activation(
                stt_sc[:, sb + 6:sb + 7], stt_sc[:, sb + 5:sb + 6], AF.Exp, scale=cst[:, 2:3]), inc="a")
            if own:
                v1 = st("vector", lambda v, xa=xa, par=par, sb=sb: v.tensor_scalar(
                    xn2[par][:], xa[:], stt_sc[:, sb + 2:sb + 3], stt_sc[:, sb + 6:sb + 7],
                    ALU.subtract, ALU.mult),
                    waits=[("a", a2), ("gdma", xnsp_hist[par])], inc="v")
                v1 = st("vector", lambda v, par=par: v.tensor_copy(
                    xnb_scr2[par][:], xn2[par][:]),
                    waits=[("trdma", tr_hist[par])], inc="v")
                tdst = xnT3[:, :, 128 * ti:128 * (ti + 1)]
            else:
                v1 = st("vector", lambda v, xa=xa, par=par, sb=sb: v.tensor_scalar(
                    xnb_scr2[par][:], xa[:], stt_sc[:, sb + 2:sb + 3], stt_sc[:, sb + 6:sb + 7],
                    ALU.subtract, ALU.mult), waits=[("a", a2), ("trdma", tr_hist[par])], inc="v")
                tdst = xnoT[par][:].rearrange("p (c l) -> p c l", c=8)
            x_hist[par] = (v1, a1)
            tr1 = st("scalar", lambda s, tdst=tdst, par=par: s.dma_start(
                out=tdst, in_=xnb_scr2[par][:], transpose=True),
                waits=[("v", v1), ("pe", xnoT_pe[par])], inc="trdma")
            tr_hist[par] = tr1
            xsrcT = xnT3[:, :, 128 * ti:128 * (ti + 1)] if own else xnoT[par][:].rearrange("p (c l) -> p c l", c=8)

            pe0 = 0
            w0 = [("trdma", tr1), ("gdma", ldw1), ("v", kv_evac["k"])]
            for c in range(8):
                for half in (0, 1):
                    pe0 = st("tensor", lambda t, c=c, half=half, xsrcT=xsrcT: t.matmul(
                        ps_k[:, 512 * half:512 * (half + 1)], xsrcT[:, c, :],
                        wkv[:, 1024 * c + 512 * half:1024 * c + 512 * (half + 1)],
                        start=(c == 0), stop=False), waits=w0, inc="pe")
                    w0 = []
            pe0 = st("tensor", lambda t: t.matmul(ps_k[:, 0:512], ones1[0:1, 0:128], brows[0:1, H:H + 512],
                                                  start=False, stop=True), inc="pe")
            pe_k = st("tensor", lambda t: t.matmul(ps_k[:, 512:1024], ones1[0:1, 0:128], brows[0:1, H + 512:2 * H],
                                                   start=False, stop=True), inc="pe")
            w0 = [("v", kv_evac["v"])]
            for c in range(8):
                for half in (0, 1):
                    pe0 = st("tensor", lambda t, c=c, half=half, xsrcT=xsrcT: t.matmul(
                        ps_v[:, 512 * half:512 * (half + 1)], xsrcT[:, c, :],
                        wkv[:, 8192 + 1024 * c + 512 * half:8192 + 1024 * c + 512 * (half + 1)],
                        start=(c == 0), stop=False), waits=w0, inc="pe")
                    w0 = []
            pe0 = st("tensor", lambda t: t.matmul(ps_v[:, 0:512], ones1[0:1, 0:128], brows[0:1, 2 * H:2 * H + 512],
                                                  start=False, stop=True), inc="pe")
            pe_kv = st("tensor", lambda t: t.matmul(ps_v[:, 512:1024], ones1[0:1, 0:128], brows[0:1, 2 * H + 512:3 * H],
                                                    start=False, stop=True), inc="pe")
            if not own:
                xnoT_pe[par] = pe_kv
            a3 = st("scalar", lambda s, eo=eo: s.activation(escr[:, eo:eo + 1024], ps_k[:], AF.Exp),
                    waits=[("pe", pe_k)], inc="a")
            v2 = st("vector", lambda v, eo=eo: v.tensor_single_scalar(
                escr[:, eo:eo + 1024], escr[:, eo:eo + 1024], 1.0, ALU.min),
                waits=[("a", a3), ("pe", kevac_gate[par])], inc="v")
            v2 = st("vector", lambda v, par=par, eo=eo: v.scalar_tensor_tensor(
                ke_t[par][:], ps_k[:], 0.0, escr[:, eo:eo + 1024], ALU.max, ALU.add), inc="v")
            kv_evac["k"] = v2
            v3 = st("vector", lambda v, par=par: v.tensor_copy(
                ve_t[par][:].rearrange("p (h c) -> p h c", h=NH)[:, :, 0:64],
                ps_v[:].rearrange("p (h c) -> p h c", h=NH)),
                waits=[("pe", pe_kv)], inc="v")
            kv_evac["v"] = v3
            for hh in range(NH):
                pr = hh // 2
                ro = 64 * (hh % 2)
                dst = ps_c if pr < 4 else ps_d
                co = 65 * (pr % 4)
                pkv = st("tensor", lambda t, hh=hh, ro=ro, dst=dst, co=co, par=par, tt=tt: t.matmul(
                    dst[ro:ro + 64, co:co + 65], ke_t[par][:, 64 * hh:64 * hh + 64],
                    ve_t[par][:, 65 * hh:65 * hh + 65],
                    start=(tt == 0), stop=(tt == NTF - 1)),
                    waits=[("v", v3)] if hh == 0 else [], inc="pe")
            kevac_gate[par] = pkv

            if own:
                v5 = st("vector", lambda v, par=par, ti=ti: v.tensor_reduce(
                    amax1[:, ti:ti + 1], xn2[par][:], AX,
                    ALU.max, apply_absolute_value=True), inc="v")
                SPV()
                v5 = st("vector", lambda v, ti=ti: v.tensor_single_scalar(
                    inv1[:, ti:ti + 1], amax1[:, ti:ti + 1], 1e-5, ALU.max), inc="v")
                SPV()
                v5 = st("vector", lambda v, ti=ti: v.reciprocal(s1c[:, ti:ti + 1], inv1[:, ti:ti + 1]), inc="v")
                SPV()
                v5 = st("vector", lambda v, ti=ti: v.tensor_single_scalar(
                    s1c[:, ti:ti + 1], s1c[:, ti:ti + 1], 127.0, ALU.mult), inc="v")
                v5 = st("vector", lambda v, ti=ti: v.tensor_single_scalar(
                    inv1[:, ti:ti + 1], inv1[:, ti:ti + 1], 1.0 / 127.0, ALU.mult), inc="v")
                v5 = st("vector", lambda v, par=par, ti=ti: v.tensor_scalar(
                    q3scr[:], xn2[par][:], s1c[:, ti:ti + 1], MAGIC,
                    ALU.mult, ALU.add), inc="v")
                v5 = st("vector", lambda v, par=par: v.tensor_single_scalar(
                    n1b2[par][:], q3scr[:], MAGIC, ALU.subtract),
                    waits=[("trdma", n1tr_hist[par])], inc="v")
                n1tr_hist[par] = st("scalar", lambda s, ti=ti, par=par: s.dma_start(
                    out=n1T3[:, :, 128 * ti:128 * (ti + 1)], in_=n1b2[par][:], transpose=True),
                    waits=[("v", v5)], inc="trdma")
                xnsp_hist[par] = st("gpsimd", lambda g, ti=ti, par=par: g.dma_start(
                    out=xn_sp3[ti], in_=xn2[par][:]), waits=[("v", v5)], inc="gdma")

        v4 = st("vector", lambda v: v.tensor_copy(kv_sb[:, 0:260], ps_c[:, 0:260]),
                waits=[("pe", pkv)], inc="v")
        v4 = st("vector", lambda v: v.tensor_copy(kv_sb[:, 260:520], ps_d[:, 0:260]), inc="v")

        tap(0, lambda: xn2[0][:])
        tap(1, lambda: kv_sb[:, 0:520])
        b.barrier()

        xn_sp_done = b.n["gdma"]
        st("vector", lambda v: v.tensor_scalar(cg_all[:], inv1[:], mcol[:, 1:2], None, ALU.mult), inc="v")
        st("vector", lambda v: v.tensor_scalar(cu_all[:], inv1[:], mcol[:, 2:3], None, ALU.mult), inc="v")

        tap(3, lambda: n1T[:, 0:1024])
        tap(9, lambda: amax1[:, 0:NT])
        tap(10, lambda: s1c[:, 0:NT])
        b.barrier()

        ldq = st("gpsimd", lambda g: g.dma_start(
            out=wq2[:].rearrange("p (c n) -> p c n", c=8),
            in_=dr["wq_t"].rearrange("(c p) n -> p c n", p=128)), inc="gdma")
        v6 = st("vector", lambda v: v.tensor_copy(kv_bf[:], kv_sb[:]), inc="v")
        v6 = st("vector", lambda v: v.memset(dstat[:], 0.0), inc="v")
        for hh in range(NH):
            pr, ro = hh // 2, 64 * (hh % 2)
            cof = 260 * (pr // 4) + 65 * (pr % 4) + 64
            v6 = st("vector", lambda v, hh=hh, ro=ro, cof=cof: v.tensor_single_scalar(
                dstat[ro:ro + 64, hh:hh + 1], kv_sb[ro:ro + 64, cof:cof + 1], EPS_ATTN, ALU.add), inc="v")
        pe1 = 0
        for half in (0, 1):
            pe1 = st("tensor", lambda t, half=half: t.matmul(
                ps_k[:, 512 * half:512 * (half + 1)], ones1[0:1, 0:128],
                brows[0:1, 4 * H + 512 * half:4 * H + 512 * (half + 1)],
                start=True, stop=True), waits=[("v", v6)] if half == 0 else [], inc="pe")
        v6 = st("vector", lambda v: v.tensor_copy(gb_bc[:, 0:H], ps_k[:, 0:H]),
                waits=[("pe", pe1)], inc="v")
        for half in (0, 1):
            pe1 = st("tensor", lambda t, half=half: t.matmul(
                ps_k[:, 512 * half:512 * (half + 1)], ones1[0:1, 0:128],
                brows[0:1, 5 * H + 512 * half:5 * H + 512 * (half + 1)],
                start=True, stop=True), waits=[("v", v6)] if half == 0 else [], inc="pe")
        v6 = st("vector", lambda v: v.tensor_copy(gb_bc[:, H:2 * H], ps_k[:, 0:H]),
                waits=[("pe", pe1)], inc="v")

        prev_ctx_evac = 0
        a_ln_prev = 0
        zpair_free = 0
        denom_ps = lambda sub: (ps_k if sub < 2 else ps_v)
        for c in range(8):
            peq = 0
            w0 = [("gdma", ldq), ("v", prev_ctx_evac if c else v6)]
            for fin in range(8):
                for sub in range(4):
                    psd = ps_c if sub < 2 else ps_d
                    so = 512 * (sub % 2)
                    peq = st("tensor", lambda t, c=c, fin=fin, sub=sub, psd=psd, so=so: t.matmul(
                        psd[:, so:so + 512], wq2[:, 1024 * fin + 128 * c:1024 * fin + 128 * (c + 1)],
                        xnT3[:, fin, 512 * sub:512 * (sub + 1)],
                        start=(fin == 0), stop=False), waits=w0, inc="pe")
                    w0 = []
            for sub in range(4):
                psd = ps_c if sub < 2 else ps_d
                so = 512 * (sub % 2)
                peq = st("tensor", lambda t, c=c, sub=sub, psd=psd, so=so: t.matmul(
                    psd[:, so:so + 512], brows[0:1, 128 * c:128 * (c + 1)], ones1[0:1, 0:512],
                    start=False, stop=True), inc="pe")
            a4 = 0
            for sub in range(4):
                psd = ps_c if sub < 2 else ps_d
                so = 512 * (sub % 2)
                a4 = st("scalar", lambda s, sub=sub, psd=psd, so=so: s.activation(
                    escr[:, 512 * sub:512 * (sub + 1)], psd[:, so:so + 512], AF.Exp),
                    waits=[("pe", peq)] if sub == 0 else [], inc="a")
            v7 = st("vector", lambda v: v.tensor_single_scalar(
                escr[:, 0:2048], escr[:, 0:2048], 1.0, ALU.min), waits=[("a", a4)], inc="v")
            for sub in range(4):
                psd = ps_c if sub < 2 else ps_d
                so = 512 * (sub % 2)
                v7 = st("vector", lambda v, sub=sub, psd=psd, so=so: v.scalar_tensor_tensor(
                    qe_c[:, 512 * sub:512 * (sub + 1)], psd[:, so:so + 512], 0.0,
                    escr[:, 512 * sub:512 * (sub + 1)], ALU.max, ALU.add), inc="v")
            ped = 0
            for sub in range(4):
                dps = denom_ps(sub)
                so = 512 * (sub % 2)
                for half in (0, 1):
                    ped = st("tensor", lambda t, c=c, sub=sub, dps=dps, so=so, half=half: t.matmul(
                        dps[64 * half:64 * half + 64, so:so + 512],
                        dstat[:, 2 * c + half:2 * c + half + 1].broadcast_to([128, 64]),
                        qe_c[:, 512 * sub:512 * (sub + 1)], start=True, stop=True),
                        waits=[("v", v7), ("a", a_ln_prev)] if sub == 0 and half == 0 else [],
                        inc="pe")
            a_ln = 0
            for sub in range(4):
                dps = denom_ps(sub)
                so = 512 * (sub % 2)
                a_ln = st("scalar", lambda s, sub=sub, dps=dps, so=so: s.activation(
                    zl_buf[:, 512 * sub:512 * (sub + 1)], dps[:, so:so + 512], AF.Ln),
                    waits=[("pe", ped)] if sub == 0 else [], inc="a")
            a_ln_prev = a_ln
            a_z = st("scalar", lambda s: s.activation(
                zpair[:], zl_buf[:], AF.Exp, scale=cst[:, 3:4]),
                waits=[("v", zpair_free)], inc="a")
            pectx = 0
            cof = 260 * (c // 4) + 65 * (c % 4)
            for half in (0, 1):
                for sub in range(4):
                    psd = ps_c if sub < 2 else ps_d
                    so = 512 * (sub % 2)
                    pectx = st("tensor", lambda t, half=half, cof=cof, sub=sub, psd=psd, so=so: t.matmul(
                        psd[64 * half:64 * half + 64, so:so + 512],
                        kv_bf[64 * half:64 * half + 64, cof:cof + 64],
                        qe_c[64 * half:64 * half + 64, 512 * sub:512 * (sub + 1)],
                        start=True, stop=True), inc="pe")
            v8 = 0
            for sub in range(4):
                psd = ps_c if sub < 2 else ps_d
                so = 512 * (sub % 2)
                v8 = st("vector", lambda v, c=c, sub=sub, psd=psd, so=so: v.scalar_tensor_tensor(
                    ctxT3[:, c, 512 * sub:512 * (sub + 1)], psd[:, so:so + 512], 1.0,
                    zpair[:, 512 * sub:512 * (sub + 1)], ALU.mult, ALU.mult),
                    waits=[("pe", pectx), ("a", a_z)] if sub == 0 else [], inc="v")
            prev_ctx_evac = v8
            zpair_free = v8

        prev_zmul = prev_ctx_evac

        ldd = st("gpsimd", lambda g: g.dma_start(
            out=wq2[:].rearrange("p (c n) -> p c n", c=8),
            in_=dr["wd_t"].rearrange("(c p) n -> p c n", p=128)),
            waits=[("pe", b.n["pe"])], inc="gdma")

        prev_s_sp = {0: 0, 1: 0}
        hs_gate = {0: 0, 1: 0}
        trb_gate = {0: 0, 1: 0}
        prev_ln2_v = {0: 0, 1: 0}
        for ti in range(NT):
            par = ti % 2
            sb = 8 * par
            psd_t = ps_k if par == 0 else ps_v
            trb = st("sync", lambda s, ti=ti, par=par: s.dma_start(
                out=xnb_back[par][:], in_=xn_sp3[ti]),
                waits=[("v", trb_gate[par]), ("gdma", xn_sp_done)], inc="ldma")
            ped2 = 0
            w0 = [("gdma", ldd), ("v", prev_zmul), ("v", hs_gate[par])]
            for c in range(8):
                for half in (0, 1):
                    ped2 = st("tensor", lambda t, ti=ti, c=c, half=half, psd_t=psd_t: t.matmul(
                        psd_t[:, 512 * half:512 * (half + 1)],
                        ctxT3[:, c, 128 * ti:128 * (ti + 1)],
                        wq2[:, 1024 * c + 512 * half:1024 * c + 512 * (half + 1)],
                        start=(c == 0), stop=False), waits=w0, inc="pe")
                    w0 = []
            for half in (0, 1):
                ped2 = st("tensor", lambda t, half=half, psd_t=psd_t: t.matmul(
                    psd_t[:, 512 * half:512 * (half + 1)], ones1[0:1, 0:128],
                    brows[0:1, 3 * H + 512 * half:3 * H + 512 * (half + 1)],
                    start=False, stop=True), inc="pe")
            v10 = st("vector", lambda v, par=par, psd_t=psd_t: v.scalar_tensor_tensor(
                hs_scr2[par][:], psd_t[:], 1.0, xnb_back[par][:], ALU.mult, ALU.add),
                waits=[("pe", ped2), ("ldma", trb)], inc="v")
            hs_gate[par] = v10
            if DEBUG_NAN:
                st("gpsimd", lambda g, ti=ti, par=par: g.dma_start(
                    out=dr["dbg_h"].rearrange("(t p) n -> t p n", p=128)[ti], in_=hs_scr2[par][:]),
                    waits=[("v", v10)], inc="gdma")
                st("gpsimd", lambda g, ti=ti, par=par: g.dma_start(
                    out=dr["dbg_x"].rearrange("(t p) n -> t p n", p=128)[ti], in_=xnb_back[par][:]),
                    waits=[("ldma", trb)], inc="gdma")
            trb_gate[par] = v10
            a6 = st("scalar", lambda s, par=par, sb=sb: s.activation(
                act_junk[:], hs_scr2[par][:], AF.Identity, accum_out=stt_sc[:, sb:sb + 1]),
                waits=[("v", v10), ("v", prev_ln2_v[par])], inc="a")
            a6 = st("scalar", lambda s, par=par, sb=sb: s.activation(
                act_junk[:], hs_scr2[par][:], AF.Square, accum_out=stt_sc[:, sb + 1:sb + 2]), inc="a")
            v10 = st("vector", lambda v, sb=sb: v.tensor_single_scalar(
                stt_sc[:, sb + 2:sb + 3], stt_sc[:, sb:sb + 1], 1.0 / H, ALU.mult), waits=[("a", a6)], inc="v")
            v10 = st("vector", lambda v, sb=sb: v.tensor_single_scalar(
                stt_sc[:, sb + 7:sb + 8], stt_sc[:, sb + 1:sb + 2], 1.0 / H, ALU.mult), inc="v")
            v10 = st("vector", lambda v, sb=sb: v.tensor_mul(
                stt_sc[:, sb + 4:sb + 5], stt_sc[:, sb + 2:sb + 3], stt_sc[:, sb + 2:sb + 3]), inc="v")
            SPV()
            v10 = st("vector", lambda v, sb=sb: v.tensor_sub(
                stt_sc[:, sb + 3:sb + 4], stt_sc[:, sb + 7:sb + 8], stt_sc[:, sb + 4:sb + 5]), inc="v")
            prev_ln2_v[par] = v10
            a6 = st("scalar", lambda s, sb=sb: s.activation(
                stt_sc[:, sb + 5:sb + 6], stt_sc[:, sb + 3:sb + 4], AF.Ln, bias=cst[:, 1:2]),
                waits=[("v", v10)], inc="a")
            SPA()
            a6 = st("scalar", lambda s, sb=sb: s.activation(
                stt_sc[:, sb + 6:sb + 7], stt_sc[:, sb + 5:sb + 6], AF.Exp, scale=cst[:, 2:3]), inc="a")
            if DEBUG_NAN:
                st("gpsimd", lambda g, ti=ti: g.dma_start(out=dr["dbg_t"][ti], in_=stt_sc[:]),
                    waits=[("a", a6)], inc="gdma")
            v11 = st("vector", lambda v, par=par, sb=sb: v.tensor_scalar(
                hs_scr2[par][:], hs_scr2[par][:], stt_sc[:, sb + 2:sb + 3], stt_sc[:, sb + 6:sb + 7],
                ALU.subtract, ALU.mult),
                waits=[("a", a6)], inc="v")
            v11 = st("vector", lambda v, par=par: v.tensor_mul(
                hs_scr2[par][:], hs_scr2[par][:], gb_bc[:, 0:H]), inc="v")
            v11 = st("vector", lambda v, par=par: v.tensor_add(
                s_tile2[par][:], hs_scr2[par][:], gb_bc[:, H:2 * H]),
                waits=[("gdma", prev_s_sp[par])], inc="v")
            prev_s_sp[par] = st("gpsimd", lambda g, ti=ti, par=par: g.dma_start(
                out=s_sp3[ti], in_=s_tile2[par][:]), waits=[("v", v11)], inc="gdma")
            if DEBUG_NAN:
                prev_s_sp[par] = st("gpsimd", lambda g, ti=ti, par=par: g.dma_start(
                    out=dr["dbg_s"].rearrange("(t p) n -> t p n", p=128)[ti], in_=s_tile2[par][:]),
                    waits=[("v", v11)], inc="gdma")

        tap(4, lambda: qe_c[:, 0:1024])
        tap(5, lambda: zl_buf[:, 0:1024])
        tap(6, lambda: ctxT[:, 0:1024])
        tap(7, lambda: hs_scr2[0][:, 0:1024])
        tap(11, lambda: dstat[:, 0:128])
        if DEBUG_NAN:
            st("gpsimd", lambda g: g.dma_start(out=dr["dbg_n"], in_=n1T[:]),
               waits=[("trdma", b.n["trdma"])], inc="gdma")
            st("gpsimd", lambda g: g.dma_start(out=dr["dbg_c"], in_=ctxT[:]),
               waits=[("v", b.n["v"])], inc="gdma")
            st("gpsimd", lambda g: g.dma_start(out=dr["dbg_i"][:, 0:16], in_=inv1[:]),
               waits=[("v", b.n["v"])], inc="gdma")
            st("gpsimd", lambda g: g.dma_start(out=dr["dbg_i"][:, 16:32], in_=cg_all[:]), inc="gdma")
            st("gpsimd", lambda g: g.dma_start(out=dr["dbg_i"][:, 32:48], in_=cu_all[:]), inc="gdma")
            st("gpsimd", lambda g: g.dma_start(out=dr["dbg_i"][:, 48:64], in_=mcol[:]), inc="gdma")
        b.barrier()

        tern_g3 = dr["tern_g"].rearrange("(c p) n -> p c n", p=128)
        tern_u3 = dr["tern_u"].rearrange("(c p) n -> p c n", p=128)
        tern_d4 = dr["tern_d"].rearrange("(c p) n -> p c n", p=128)
        wgu_used = {0: 0, 1: 0}
        down_used = {0: 0, 1: 0}
        evac_gate = {"a": 0, "v": 0}
        a8_hist = {0: 0, 1: 0}
        v14_hist = {0: 0, 1: 0}
        st_hist = {0: 0, 1: 0}
        ped3 = 0
        for g in range(NGRP):
            for iq in range(4):
                ldg_g = st("sync", lambda s, iq=iq: s.dma_start(
                    out=wgu[:, 0:8 * 1024].rearrange("p (c n) -> p c n", c=8),
                    in_=tern_g3[:, :, 1024 * iq:1024 * (iq + 1)]),
                    waits=[("pe", wgu_used[0])], inc="ldma")
                ldg_u = st("scalar", lambda s, iq=iq: s.dma_start(
                    out=wgu[:, 8 * 1024:16 * 1024].rearrange("p (c n) -> p c n", c=8),
                    in_=tern_u3[:, :, 1024 * iq:1024 * (iq + 1)]),
                    waits=[("pe", wgu_used[0])], inc="tdma")
                for tl in range(GT):
                    ti = g * GT + tl
                    peg = 0
                    w0 = [("ldma", ldg_g), ("tdma", ldg_u),
                          ("a", evac_gate["a"]), ("v", evac_gate["v"])]
                    for c in range(8):
                        for (psd, wof) in ((ps_k, 0), (ps_v, 8192)):
                            for half in (0, 1):
                                peg = st("tensor", lambda t, ti=ti, c=c, psd=psd, wof=wof, half=half: t.matmul(
                                    psd[:, 512 * half:512 * (half + 1)],
                                    n1T3[:, c, 128 * ti:128 * (ti + 1)],
                                    wgu[:, wof + 1024 * c + 512 * half:wof + 1024 * c + 512 * (half + 1)],
                                    start=(c == 0), stop=(c == 7)), waits=w0, inc="pe")
                                w0 = []
                    if tl == GT - 1:
                        wgu_used[0] = peg
                    a7 = st("scalar", lambda s, ti=ti: s.activation(
                        sg_scr[:], ps_k[:], AF.Silu, scale=cg_all[:, ti:ti + 1]),
                        waits=[("pe", peg), ("v", evac_gate["v"])], inc="a")
                    v12 = st("vector", lambda v, ti=ti, tl=tl, iq=iq: v.scalar_tensor_tensor(
                        t3[:, 4096 * tl + 1024 * iq:4096 * tl + 1024 * (iq + 1)],
                        ps_v[:], cu_all[:, ti:ti + 1], sg_scr[:], ALU.mult, ALU.mult),
                        waits=[("a", a7)], inc="v")
                    evac_gate = {"a": a7, "v": v12}
            if DEBUG_NAN and g == 0:
                st("gpsimd", lambda g2: g2.dma_start(out=dr["dbg_t3"], in_=t3[:]),
                   waits=[("v", b.n["v"])], inc="gdma")
            for hp in range(GT // 2):
                tr4s = {}
                for par in (0, 1):
                    tl = 2 * hp + par
                    ti = g * GT + tl
                    pc = 4 * par
                    v13 = st("vector", lambda v, tl=tl, pc=pc: v.tensor_reduce(
                        p4c[:, pc:pc + 1], t3[:, 4096 * tl:4096 * (tl + 1)], AX,
                        ALU.max, apply_absolute_value=True),
                        waits=[("a", a8_hist[par])], inc="v")
                    SPV()
                    v13 = st("vector", lambda v, pc=pc: v.tensor_single_scalar(
                        p4c[:, pc + 1:pc + 2], p4c[:, pc:pc + 1], 1e-5, ALU.max), inc="v")
                    SPV()
                    v13 = st("vector", lambda v, pc=pc: v.reciprocal(
                        p4c[:, pc + 2:pc + 3], p4c[:, pc + 1:pc + 2]), inc="v")
                    SPV()
                    v13 = st("vector", lambda v, pc=pc: v.tensor_single_scalar(
                        p4c[:, pc + 2:pc + 3], p4c[:, pc + 2:pc + 3], 127.0, ALU.mult), inc="v")
                    v13 = st("vector", lambda v, pc=pc: v.tensor_scalar(
                        p4c[:, pc + 3:pc + 4], p4c[:, pc + 1:pc + 2], mcol[:, 3:4], 1.0 / 127.0,
                        ALU.mult, ALU.mult), inc="v")
                    SPV()
                    trw = b.n["trdma"]
                    for qs in range(4):
                        v13 = st("vector", lambda v, tl=tl, pc=pc, qs=qs: v.tensor_scalar(
                            y2_scr[:], t3[:, 4096 * tl + 1024 * qs:4096 * tl + 1024 * (qs + 1)],
                            p4c[:, pc + 2:pc + 3], MAGIC, ALU.mult, ALU.add),
                            waits=[("trdma", trw)] if qs == 0 else [], inc="v")
                        v13 = st("vector", lambda v, qs=qs: v.tensor_single_scalar(
                            n2tok[:, 1024 * qs:1024 * (qs + 1)], y2_scr[:], MAGIC, ALU.subtract), inc="v")
                    tr4s[par] = st("scalar", lambda s, par=par: s.dma_start(
                        out=n2T[par][:].rearrange("p (c l) -> p c l", c=32),
                        in_=n2tok[:], transpose=True),
                        waits=[("v", v13), ("pe", down_used[par])], inc="trdma")
                ped_hist = {0: ped3, 1: ped3}
                for cpair in range(16):
                    cpar = cpair % 2
                    ring, rsem = ("sync", "ldma") if cpar == 0 else ("gpsimd", "gdma")
                    ldd2 = st(ring, lambda s, cpair=cpair, cpar=cpar: s.dma_start(
                        out=tdc[cpar][:].rearrange("p (c n) -> p c n", c=2),
                        in_=tern_d4[:, 2 * cpair:2 * cpair + 2, :]),
                        waits=[("pe", ped_hist[cpar])], inc=rsem)
                    for ci in (0, 1):
                        c = 2 * cpair + ci
                        for par in (0, 1):
                            psd = ps_c if par == 0 else ps_d
                            for half in (0, 1):
                                ped3 = st("tensor", lambda t, c=c, ci=ci, cpar=cpar, par=par, half=half, psd=psd: t.matmul(
                                    psd[:, 512 * half:512 * (half + 1)],
                                    n2T[par][:, 128 * c:128 * (c + 1)],
                                    tdc[cpar][:, 1024 * ci + 512 * half:1024 * ci + 512 * (half + 1)],
                                    start=(c == 0), stop=(c == 31)),
                                    waits=([("trdma", tr4s[par]), (rsem, ldd2),
                                            ("a", a8_hist[0] if c == 0 else 0),
                                            ("a", a8_hist[1] if c == 0 else 0)]
                                           if half == 0 and ci == 0 else []),
                                    inc="pe")
                    ped_hist[cpar] = ped3
                for par in (0, 1):
                    down_used[par] = ped3
                for par in (0, 1):
                    tl = 2 * hp + par
                    ti = g * GT + tl
                    pc = 4 * par
                    psd = ps_c if par == 0 else ps_d
                    ldx = st("sync", lambda s, ti=ti, par=par: s.dma_start(out=xnb_back[par][:], in_=x_own3[ti]),
                             waits=[("v", v14_hist[par])], inc="ldma")
                    lds = st("gpsimd", lambda g2, ti=ti, par=par: g2.dma_start(out=s_tile2[par][:], in_=s_sp3[ti]),
                             waits=[("v", v14_hist[par])], inc="gdma")
                    a8 = st("scalar", lambda s, par=par, pc=pc, psd=psd: s.activation(
                        act_junk[:], psd[:], AF.Silu, scale=p4c[:, pc + 3:pc + 4]),
                        waits=[("pe", ped3), ("v", v14_hist[0]), ("v", v14_hist[1])], inc="a")
                    a8_hist[par] = a8
                    if DEBUG_NAN:
                        st("gpsimd", lambda g2, ti=ti: g2.dma_start(
                            out=dr["dbg_g"].rearrange("(t p) n -> t p n", p=128)[ti], in_=act_junk[:]),
                            waits=[("a", a8)], inc="gdma")
                    v14 = st("vector", lambda v, par=par: v.tensor_mul(act_junk[:], act_junk[:], s_tile2[par][:]),
                             waits=[("a", a8), ("gdma", lds)], inc="v")
                    v14 = st("vector", lambda v, par=par: v.tensor_add(hs_scr2[par][:], act_junk[:], xnb_back[par][:]),
                             waits=[("ldma", ldx), ("stdma", st_hist[par])], inc="v")
                    v14_hist[par] = v14
                    st_hist[par] = st("sync", lambda s, ti=ti, par=par: s.dma_start(
                        out=out3[ti], in_=hs_scr2[par][:]), waits=[("v", v14)], inc="stdma")

        tap(8, lambda: t3[:, 0:1024])
        b.steps.append(Step("sync", None, [("stdma", b.n["stdma"])], None))

        emit(b, nc)
    return nc


def emit(b, nc):
    sem_handles = {}
    with ExitStack() as ctx:
        for s in SEMS:
            sem_handles[s] = ctx.enter_context(nc.semaphore(s))
        with nc.Block() as block:
            for name in ENGINES:
                section = getattr(block, name)

                def body(eng, name=name):
                    for stp in b.steps:
                        if stp.engine != name:
                            continue
                        for (s, val) in stp.waits:
                            eng.wait_ge(sem_handles[s], val)
                        if stp.emit is None:
                            continue
                        inst = stp.emit(eng)
                        if stp.incs is not None and inst is not None:
                            inst.then_inc(sem_handles[stp.incs[0]], stp.incs[1])

                section(body)


def kernel(**inputs):
    x = np.ascontiguousarray(np.asarray(inputs["L0_attn_input_tensor"], dtype=np.float32))
    row = lambda a: np.ascontiguousarray(np.asarray(a, np.float32)).reshape(1, -1)
    tr = lambda a: np.ascontiguousarray(np.asarray(a, np.float32).T)
    shared = {
        "wq_t": tr(inputs["L0_attn_Wq"]), "wk_t": tr(inputs["L0_attn_Wk"]),
        "wv_t": tr(inputs["L0_attn_Wv"]), "wd_t": tr(inputs["L0_attn_Wd"]),
        "wg_t": tr(inputs["L0_ffn_w_gate"]), "wu_t": tr(inputs["L0_ffn_w_up"]),
        "wdn_t": tr(inputs["L0_ffn_w_down"]),
        "bq": row(inputs["L0_attn_bq"]), "bk": row(inputs["L0_attn_bk"]),
        "bv": row(inputs["L0_attn_bv"]), "bd": row(inputs["L0_attn_bd"]),
        "gamma": row(inputs["L0_attn_gamma"]), "beta": row(inputs["L0_attn_beta"]),
    }
    in_maps = []
    for c in range(8):
        n, h = c // 2, c % 2
        m = dict(shared)
        m["x_own"] = np.ascontiguousarray(x[n, h * TOWN:(h + 1) * TOWN])
        m["x_oth"] = np.ascontiguousarray(x[n, (1 - h) * TOWN:(2 - h) * TOWN])
        in_maps.append(m)
    nc = build_core_program()
    res = run_bass_kernel_spmd(nc, in_maps, core_ids=list(range(8)))
    out = np.empty((4, 4096, 1024), np.float32)
    for c in range(8):
        n, h = c // 2, c % 2
        out[n, h * TOWN:(h + 1) * TOWN] = res.results[c]["out"]
    return out
